# version 02
# baseline (speedup 1.0000x reference)
import numpy as np
import ml_dtypes
from contextlib import ExitStack

import concourse.bass as bass
import concourse.mybir as mybir
from concourse.bass_utils import run_bass_kernel_spmd

F32 = mybir.dt.float32
BF16 = mybir.dt.bfloat16
AF = mybir.ActivationFunctionType
ALU = mybir.AluOpType
AX = mybir.AxisListType.X

NCORES = 8
TC = 2048
D = 2048
NDG = D // 128
NST = 16
BF = ml_dtypes.bfloat16

ENGS = ("PE", "ACT", "DVE", "POOL", "SP")


class Reg:

    __slots__ = ("t", "p0", "pn", "c0", "cn")

    def __init__(self, t, p0, pn, c0, cn):
        self.t, self.p0, self.pn, self.c0, self.cn = t, p0, pn, c0, cn

    @property
    def ap(self):
        return self.t[self.p0:self.p0 + self.pn, self.c0:self.c0 + self.cn]

    def box(self):
        return (id(self.t), self.p0, self.pn, self.c0, self.cn)

    def overlaps(self, o):
        if id(self.t) != id(o.t):
            return False
        return (self.p0 < o.p0 + o.pn and o.p0 < self.p0 + self.pn and
                self.c0 < o.c0 + o.cn and o.c0 < self.c0 + self.cn)


def R(t, p0=0, pn=None, c0=0, cn=None):
    shape = t.shape
    if pn is None:
        pn = shape[0] - p0
    if cn is None:
        cn = shape[1] - c0
    return Reg(t, p0, pn, c0, cn)


class Auto:

    def __init__(self, name="prog"):
        self.nc = bass.Bass("TRN2", target_bir_lowering=False)
        self.ctx = ExitStack()
        self._sem = {}
        for e in ENGS:
            self._sem[e] = self.ctx.enter_context(self.nc.semaphore("as_" + e))
        self.NSLOT = 12
        self._dsem = {}
        self._dslot_count = {}
        self._dslot_next = {}
        for e in ("SP", "POOL", "ACT"):
            self._dsem[e] = [self.ctx.enter_context(self.nc.semaphore(f"ad_{e}_{k}"))
                             for k in range(self.NSLOT)]
            self._dslot_count[e] = [0] * self.NSLOT
            self._dslot_next[e] = 0
        self.insts = []
        self.writes = {}
        self.reads = {}
        self.warned = set()

    def sbuf(self, name, shape, dtype):
        return self.ctx.enter_context(self.nc.sbuf_tensor(name, list(shape), dtype))

    def psum(self, name, shape, dtype=F32):
        return self.ctx.enter_context(self.nc.psum_tensor(name, list(shape), dtype))

    def inp(self, name, shape, dtype=F32):
        return self.nc.dram_tensor(name, list(shape), dtype, kind="ExternalInput").ap()

    def out(self, name, shape, dtype=F32):
        return self.nc.dram_tensor(name, list(shape), dtype, kind="ExternalOutput").ap()

    def dram(self, name, shape, dtype=F32):
        return self.nc.dram_tensor(name, list(shape), dtype, kind="Internal").ap()

    def op(self, engine, fn, reads=(), writes=(), sreads=(), _nohop=False):
        self._emit(engine, fn, reads, writes, sreads, kind="op", nohop=_nohop)

    def dma(self, queue, out, in_, reads=(), writes=()):
        reads = list(reads)
        writes = list(writes)
        oa = out.ap if isinstance(out, Reg) else out
        ia = in_.ap if isinstance(in_, Reg) else in_
        if isinstance(out, Reg):
            writes.append(out)
        if isinstance(in_, Reg):
            reads.append(in_)

        def fn(e):
            return e.dma_start(out=oa, in_=ia)
        self._emit(queue, fn, reads, writes, sreads=(), kind="dma")

    def fence(self):
        if not hasattr(self, "_fence_t"):
            self._fence_t = self.sbuf("fence_t", [1, 1], F32)
        waits = {}
        for i, ent in enumerate(self.insts):
            if ent["kind"] == "dma":
                waits[(ent["engine"], "dma", ent["slot"])] = i
        for key, widx in waits.items():
            self.insts[widx]["inc"] = True
        ent = dict(engine="POOL", fn=lambda g: g.memset(self._fence_t[:], 0.0),
                   kind="op", waits=waits, idx=len(self.insts), inc=False,
                   inc_no=None)
        self.insts.append(ent)

    PARANOID = ()
    PARANOID_SRC = ()

    SMALL = 128

    def _hop_if_small_same_engine(self, engine, kind, reads, writes, sreads=()):
        if kind != "op":
            return
        need = {}
        for r in list(reads) + list(writes):
            for (reg, widx) in self.writes.get(id(r.t), []):
                if not reg.overlaps(r):
                    continue
                dep = self.insts[widx]
                if dep["engine"] == engine and dep["kind"] == "op" \
                        and dep.get("small_out") \
                        and dep["engine"] in ("DVE", "ACT", "POOL"):
                    need[(id(reg.t), reg.p0, reg.c0)] = reg
        for r in sreads:
            for (reg, widx) in self.writes.get(id(r.t), []):
                if not reg.overlaps(r):
                    continue
                dep = self.insts[widx]
                if dep["engine"] == engine and dep["kind"] == "op":
                    need[(id(reg.t), reg.p0, reg.c0)] = reg
        for reg in need.values():
            is_psum = "PSum" in type(reg.t).__name__
            if is_psum:
                helper = "ACT" if engine != "ACT" else "DVE"
            else:
                helper = "POOL" if engine != "POOL" else "ACT"
            if helper == "POOL":
                self.op(helper, lambda g, reg=reg: g.tensor_copy(reg.ap, reg.ap),
                        reads=[reg], writes=[reg], _nohop=True)
            elif helper == "DVE":
                self.op(helper, lambda v, reg=reg: v.tensor_copy(reg.ap, reg.ap),
                        reads=[reg], writes=[reg], _nohop=True)
            else:
                self.op(helper, lambda sc, reg=reg: sc.copy(reg.ap, reg.ap),
                        reads=[reg], writes=[reg], _nohop=True)

    def _emit(self, engine, fn, reads, writes, sreads, kind, nohop=False):
        if not nohop:
            self._hop_if_small_same_engine(engine, kind, reads, writes, sreads)
        idx = len(self.insts)
        waits = {}
        if engine in Auto.PARANOID:
            last = {}
            for i, e in enumerate(self.insts):
                last[(e["engine"], e["kind"])] = i
            for key, widx in last.items():
                if not (key[0] == engine and key[1] == kind == "op"):
                    if Auto.PARANOID_SRC and key[0] not in Auto.PARANOID_SRC:
                        continue
                    dep = self.insts[widx]
                    waits[(key[0], key[1], dep.get("slot"))] = widx

        def add_dep(widx):
            if widx is None:
                return
            dep = self.insts[widx]
            if dep["engine"] == engine and dep["kind"] == "op" and kind == "op":
                return
            key = (dep["engine"], dep["kind"], dep.get("slot"))
            waits[key] = max(waits.get(key, -1), widx)

        for r in list(reads) + list(sreads):
            for (reg, widx) in self.writes.get(id(r.t), []):
                if reg.overlaps(r):
                    add_dep(widx)
        for w in writes:
            for (reg, widx) in self.writes.get(id(w.t), []):
                if reg.overlaps(w):
                    add_dep(widx)
            for (reg, ridx) in self.reads.get(id(w.t), []):
                if reg.overlaps(w):
                    add_dep(ridx)
        for r in sreads:
            for (reg, widx) in self.writes.get(id(r.t), []):
                if reg.overlaps(r) and self.insts[widx]["engine"] == engine \
                        and self.insts[widx]["kind"] == "op":
                    key = ("scalar-same-engine", engine)
                    if key not in self.warned:
                        self.warned.add(key)
                        print(f"AUTO-SYNC WARNING: scalar-AP read on {engine} "
                              f"produced by same engine (inst {widx}) - unsafe!")
        ent = dict(engine=engine, fn=fn, kind=kind, waits=waits, idx=idx,
                   inc=(kind == "dma"), inc_no=None,
                   small_out=any(w.cn < Auto.SMALL for w in writes))
        if kind == "dma":
            slot = self._dslot_next[engine]
            self._dslot_next[engine] = (slot + 1) % self.NSLOT
            self._dslot_count[engine][slot] += 1
            ent["slot"] = slot
            ent["slot_no"] = self._dslot_count[engine][slot]
        self.insts.append(ent)
        for key, widx in waits.items():
            self.insts[widx]["inc"] = True
        def covers(w, reg):
            return (w.p0 <= reg.p0 and w.p0 + w.pn >= reg.p0 + reg.pn and
                    w.c0 <= reg.c0 and w.c0 + w.cn >= reg.c0 + reg.cn)

        for w in writes:
            lst = self.writes.setdefault(id(w.t), [])
            lst[:] = [(reg, i) for (reg, i) in lst if not covers(w, reg)]
            lst.append((w, idx))
            rl = self.reads.setdefault(id(w.t), [])
            rl[:] = [(reg, i) for (reg, i) in rl if not covers(w, reg)]
        for r in list(reads) + list(sreads):
            self.reads.setdefault(id(r.t), []).append((r, idx))

    def finalize_build(self, tail_waits=()):
        counters = {}
        for ent in self.insts:
            if ent["inc"]:
                if ent["kind"] == "dma":
                    ent["inc_no"] = ent["slot_no"]
                else:
                    key = (ent["engine"], ent["kind"])
                    counters[key] = counters.get(key, 0) + 1
                    ent["inc_no"] = counters[key]
        blk = self.ctx.enter_context(self.nc.Block())
        auto = self

        per_eng = {e: [] for e in ENGS}
        for ent in self.insts:
            per_eng[ent["engine"]].append(ent)

        def emit_stream(eng_obj, ents):
            waited = {}
            for ent in ents:
                if ent["kind"] == "dma" and ent["slot_no"] > 1:
                    k2 = (ent["engine"], "dma", ent["slot"])
                    tgt = 16 * (ent["slot_no"] - 1)
                    if waited.get(k2, -1) < tgt:
                        waited[k2] = tgt
                        eng_obj.wait_ge(auto._dsem[ent["engine"]][ent["slot"]], tgt)
                for key, widx in sorted(ent["waits"].items(), key=lambda kv: kv[1]):
                    dep = auto.insts[widx]
                    val = dep["inc_no"]
                    assert val is not None
                    if dep["kind"] == "dma":
                        sem = auto._dsem[dep["engine"]][dep["slot"]]
                        tgt = 16 * val
                        k2 = (dep["engine"], "dma", dep["slot"])
                    else:
                        sem = auto._sem[dep["engine"]]
                        tgt = val
                        k2 = (dep["engine"], dep["kind"], None)
                    if waited.get(k2, -1) >= tgt:
                        continue
                    waited[k2] = tgt
                    eng_obj.wait_ge(sem, tgt)
                inst = ent["fn"](eng_obj)
                if ent["inc"]:
                    if ent["kind"] == "dma":
                        inst.then_inc(auto._dsem[ent["engine"]][ent["slot"]], 16)
                    elif ent["engine"] in ("DVE", "ACT", "POOL", "PE"):
                        eng_obj.drain(fusable=False).then_inc(auto._sem[ent["engine"]], 1)
                    else:
                        inst.then_inc(auto._sem[ent["engine"]], 1)

        eng_map = {}

        @blk.tensor
        def _(te):
            emit_stream(te, per_eng["PE"])

        @blk.scalar
        def _(sc):
            emit_stream(sc, per_eng["ACT"])

        @blk.vector
        def _(v):
            emit_stream(v, per_eng["DVE"])

        @blk.gpsimd
        def _(g):
            emit_stream(g, per_eng["POOL"])

        @blk.sync
        def _(sy):
            emit_stream(sy, per_eng["SP"])

        self.ctx.close()
        return self.nc


def _pad_cols(a, total):
    if a.shape[1] == total:
        return a
    out = np.zeros((a.shape[0], total), a.dtype)
    out[:, :a.shape[1]] = a
    return out


def _interleave_heads(w, hd, hd_pad):
    Dd, tot = w.shape
    H = tot // hd
    out = np.zeros((Dd, H * hd_pad), w.dtype)
    for h in range(H):
        out[:, h * hd_pad:h * hd_pad + hd] = w[:, h * hd:(h + 1) * hd]
    return out


def _lhsT_slab(w):
    return np.ascontiguousarray(w)




def build_launch1():
    A = Auto("l1")
    TT = 512
    NT = TC // TT

    xT = A.inp("xT", [D, TC + 3])
    cw = A.inp("cw", [128, NDG * 4])
    cb = A.inp("cb", [128, NDG])
    xp1 = A.inp("xp1", [D, 128])
    xp2 = A.inp("xp2", [D, 48])
    dtp = A.inp("dtp", [128, D])
    dtb = A.inp("dtb", [128, NDG])
    alog = A.inp("alog", [128, NDG * NST])
    dsk = A.inp("dsk", [128, NDG])
    onesc = A.inp("onesc", [128, 1])
    onesr = A.inp("onesr", [1, 128])
    selbc = A.inp("selbc", [48, 32 * 128], BF16)

    x1p_o = A.out("x1p", [D, TC])
    dt_o = A.out("dt", [128, TC], BF16)
    bc_o = A.out("bc", [48, TC])
    el_o = A.out("el", [128, 2 * NDG * NST])

    xbuf = A.sbuf("xbuf", [128, NDG * 515], F32)
    lnb = A.sbuf("lnb", [128, 2 * 515], F32)
    ub = A.sbuf("ub", [128, NDG * TT], F32)
    ubf = A.sbuf("ubf", [128, 2 * TT], BF16)
    sqb = A.sbuf("sqb", [128, 2 * 515], F32)
    xp1s = A.sbuf("xp1s", [128, NDG * 128], BF16)
    xp2s = A.sbuf("xp2s", [128, NDG * 48], BF16)
    dtps = A.sbuf("dtps", [128, D], BF16)
    cws = A.sbuf("cws", [128, NDG * 4], F32)
    cbs = A.sbuf("cbs", [128, NDG], F32)
    dtbs = A.sbuf("dtbs", [128, NDG], F32)
    alogs = A.sbuf("alogs", [128, NDG * NST], F32)
    amat = A.sbuf("amat", [128, NDG * NST], F32)
    dsks = A.sbuf("dsks", [128, NDG], F32)
    onescs = A.sbuf("onescs", [128, 1], F32)
    onesrs = A.sbuf("onesrs", [1, 128], F32)
    selbcs = A.sbuf("selbcs", [48, 32 * 128], BF16)
    srow = A.sbuf("srow", [1, 4 * 515], F32)
    dtbf = A.sbuf("dtbf", [128, TT], BF16)
    bctm = A.sbuf("bctm", [48, TT], F32)
    bctb = A.sbuf("bctb", [48, TT], BF16)
    brep = A.sbuf("brep", [128, NST * TT], BF16)
    crep = A.sbuf("crep", [128, NST * TT], BF16)
    deld = A.sbuf("deld", [128, TT], F32)
    dud = A.sbuf("dud", [128, TT], F32)
    sp1 = A.sbuf("sp1", [128, TT], F32)
    sp2 = A.sbuf("sp2", [128, TT], F32)
    dab = A.sbuf("dab", [128, 2 * TT], F32)
    dbub = A.sbuf("dbub", [128, 2 * TT], F32)
    hb = A.sbuf("hb", [128, 2 * TT], F32)
    ybb = A.sbuf("ybb", [128, 2 * TT], F32)
    yac = A.sbuf("yac", [128, TT], F32)
    x1t = A.sbuf("x1t", [128, 2 * TT], F32)
    sumd = A.sbuf("sumd", [128, NDG], F32)
    sd1 = A.sbuf("sd1", [128, 1], F32)
    stv = A.sbuf("stv", [128, NDG * NST], F32)
    elb = A.sbuf("elb", [128, NDG * NST], F32)

    P0 = A.psum("P0", [1, TT])
    P1 = A.psum("P1", [1, TT])
    P2a = A.psum("P2a", [128, 8])
    P2b = A.psum("P2b", [128, 8])
    P3 = A.psum("P3", [128, TT])
    P4 = A.psum("P4", [128, TT])
    P6 = A.psum("P6", [128, TT])
    P7 = A.psum("P7", [128, TT])

    for t, src in ((cws, cw), (cbs, cb), (dtbs, dtb), (alogs, alog),
                   (dsks, dsk), (onescs, onesc), (onesrs, onesr)):
        A.dma("SP", R(t), src)
    A.dma("POOL", xp1s[:, :].rearrange("p (g e) -> p g e", g=NDG),
          xp1.rearrange("(g p) e -> p g e", p=128), writes=[R(xp1s)])
    A.dma("POOL", xp2s[:, :].rearrange("p (g e) -> p g e", g=NDG),
          xp2.rearrange("(g p) e -> p g e", p=128), writes=[R(xp2s)])
    A.dma("POOL", R(dtps), dtp)
    A.dma("SP", R(selbcs), selbc)

    A.op("ACT", lambda sc: sc.activation(amat[:], alogs[:], AF.Exp),
         reads=[R(alogs)], writes=[R(amat)])
    A.op("DVE", lambda v: v.tensor_scalar_mul(amat[:], amat[:], -1.0),
         reads=[R(amat)], writes=[R(amat)])
    A.op("POOL", lambda g: g.memset(stv[:], 0.0), writes=[R(stv)])
    A.op("POOL", lambda g: g.memset(sumd[:], 0.0), writes=[R(sumd)])

    MEAN, VAR, RSTD, MR = 0, 515, 2 * 515, 3 * 515

    def rr(off, c0=0, n=512):
        return R(srow, 0, 1, off + c0, n)

    for j in range(NT):
        xlo = j * TT

        for dg in range(NDG):
            A.dma("SP", R(xbuf, 0, 128, dg * 515, 515),
                  xT[dg * 128:(dg + 1) * 128, xlo:xlo + 515])

        for dg in range(NDG):
            xr = R(xbuf, 0, 128, dg * 515, 515)
            sq = R(sqb, 0, 128, (dg % 2) * 515, 515)
            A.op("ACT", lambda sc, xr=xr, sq=sq: sc.activation(sq.ap, xr.ap, AF.Square),
                 reads=[xr], writes=[sq])
            st0, sp = dg == 0, dg == NDG - 1
            A.op("PE", lambda te, xr=xr, st0=st0, sp=sp: te.matmul(
                P0[:], onescs[:], xr.t[0:128, xr.c0:xr.c0 + 512],
                start=st0, stop=sp), reads=[xr, R(onescs)], writes=[R(P0)])
            A.op("PE", lambda te, xr=xr, st0=st0, sp=sp: te.matmul(
                P2a[0:1, 0:3], onescs[:], xr.t[0:128, xr.c0 + 512:xr.c0 + 515],
                start=st0, stop=sp), reads=[xr, R(onescs)], writes=[R(P2a, 0, 1, 0, 3)])
            A.op("PE", lambda te, sq=sq, st0=st0, sp=sp: te.matmul(
                P1[:], onescs[:], sq.t[0:128, sq.c0:sq.c0 + 512],
                start=st0, stop=sp), reads=[sq, R(onescs)], writes=[R(P1)])
            A.op("PE", lambda te, sq=sq, st0=st0, sp=sp: te.matmul(
                P2b[0:1, 0:3], onescs[:], sq.t[0:128, sq.c0 + 512:sq.c0 + 515],
                start=st0, stop=sp), reads=[sq, R(onescs)], writes=[R(P2b, 0, 1, 0, 3)])

        inv_d = 1.0 / D
        A.op("DVE", lambda v: v.tensor_scalar_mul(rr(MEAN).ap, P0[:], inv_d),
             reads=[R(P0)], writes=[rr(MEAN)])
        A.op("DVE", lambda v: v.tensor_scalar_mul(rr(MEAN, 512, 3).ap, P2a[0:1, 0:3], inv_d),
             reads=[R(P2a, 0, 1, 0, 3)], writes=[rr(MEAN, 512, 3)])
        A.op("DVE", lambda v: v.tensor_mul(rr(VAR, 0, 515).ap, rr(MEAN, 0, 515).ap,
             rr(MEAN, 0, 515).ap), reads=[rr(MEAN, 0, 515)], writes=[rr(VAR, 0, 515)])
        A.op("DVE", lambda v: v.scalar_tensor_tensor(rr(VAR).ap, P1[:], inv_d,
             rr(VAR).ap, ALU.mult, ALU.subtract), reads=[R(P1), rr(VAR)], writes=[rr(VAR)])
        A.op("DVE", lambda v: v.scalar_tensor_tensor(rr(VAR, 512, 3).ap, P2b[0:1, 0:3], inv_d,
             rr(VAR, 512, 3).ap, ALU.mult, ALU.subtract),
             reads=[R(P2b, 0, 1, 0, 3), rr(VAR, 512, 3)], writes=[rr(VAR, 512, 3)])
        A.op("DVE", lambda v: v.tensor_scalar_add(rr(VAR, 0, 515).ap, rr(VAR, 0, 515).ap,
             1e-5), reads=[rr(VAR, 0, 515)], writes=[rr(VAR, 0, 515)])
        A.op("ACT", lambda sc: sc.activation(rr(RSTD, 0, 515).ap, rr(VAR, 0, 515).ap,
             AF.Ln), reads=[rr(VAR, 0, 515)], writes=[rr(RSTD, 0, 515)])
        A.op("ACT", lambda sc: sc.activation(rr(RSTD, 0, 515).ap, rr(RSTD, 0, 515).ap,
             AF.Exp, scale=-0.5), reads=[rr(RSTD, 0, 515)], writes=[rr(RSTD, 0, 515)])
        A.op("DVE", lambda v: v.tensor_mul(rr(MR, 0, 515).ap, rr(MEAN, 0, 515).ap,
             rr(RSTD, 0, 515).ap), reads=[rr(MEAN, 0, 515), rr(RSTD, 0, 515)],
             writes=[rr(MR, 0, 515)])

        A.op("PE", lambda te: te.matmul(P3[:], onesrs[:], rr(RSTD).ap, start=True,
             stop=True), reads=[rr(RSTD), R(onesrs)], writes=[R(P3)])
        A.op("PE", lambda te: te.matmul(P2a[:, 4:7], onesrs[:], rr(RSTD, 512, 3).ap,
             start=True, stop=True), reads=[rr(RSTD, 512, 3), R(onesrs)],
             writes=[R(P2a, 0, 128, 4, 3)])
        A.op("PE", lambda te: te.matmul(P4[:], onesrs[:], rr(MR).ap, start=True,
             stop=True), reads=[rr(MR), R(onesrs)], writes=[R(P4)])
        A.op("PE", lambda te: te.matmul(P2b[:, 4:7], onesrs[:], rr(MR, 512, 3).ap,
             start=True, stop=True), reads=[rr(MR, 512, 3), R(onesrs)],
             writes=[R(P2b, 0, 128, 4, 3)])

        for dg in range(NDG):
            xr = R(xbuf, 0, 128, dg * 515, 515)
            lr = R(lnb, 0, 128, (dg % 2) * 515, 515)
            ur = R(ub, 0, 128, dg * TT, TT)
            ubr = R(ubf, 0, 128, (dg % 2) * TT, TT)
            A.op("DVE", lambda v, xr=xr, lr=lr: v.tensor_mul(
                lr.t[0:128, lr.c0:lr.c0 + 512], xr.t[0:128, xr.c0:xr.c0 + 512], P3[:]),
                reads=[xr, R(P3)], writes=[lr])
            A.op("DVE", lambda v, xr=xr, lr=lr: v.tensor_mul(
                lr.t[0:128, lr.c0 + 512:lr.c0 + 515],
                xr.t[0:128, xr.c0 + 512:xr.c0 + 515], P2a[:, 4:7]),
                reads=[xr, R(P2a, 0, 128, 4, 3)], writes=[lr])
            A.op("DVE", lambda v, lr=lr: v.tensor_sub(
                lr.t[0:128, lr.c0:lr.c0 + 512], lr.t[0:128, lr.c0:lr.c0 + 512], P4[:]),
                reads=[lr, R(P4)], writes=[lr])
            A.op("DVE", lambda v, lr=lr: v.tensor_sub(
                lr.t[0:128, lr.c0 + 512:lr.c0 + 515],
                lr.t[0:128, lr.c0 + 512:lr.c0 + 515], P2b[:, 4:7]),
                reads=[lr, R(P2b, 0, 128, 4, 3)], writes=[lr])
            w0 = R(cws, 0, 128, dg * 4, 1)
            bcol = R(cbs, 0, 128, dg, 1)
            A.op("DVE", lambda v, lr=lr, ur=ur, w0=w0, bcol=bcol: v.tensor_scalar(
                ur.ap, lr.t[0:128, lr.c0:lr.c0 + 512], w0.ap, bcol.ap, ALU.mult, ALU.add),
                reads=[lr], writes=[ur], sreads=[w0, bcol])
            for k in range(1, 4):
                wk = R(cws, 0, 128, dg * 4 + k, 1)
                A.op("DVE", lambda v, lr=lr, ur=ur, wk=wk, k=k: v.scalar_tensor_tensor(
                    ur.ap, lr.t[0:128, lr.c0 + k:lr.c0 + k + 512], wk.ap, ur.ap,
                    ALU.mult, ALU.add), reads=[lr, ur], writes=[ur], sreads=[wk])
            A.op("DVE", lambda v, ur=ur, ubr=ubr: v.tensor_copy(ubr.ap, ur.ap),
                 reads=[ur], writes=[ubr])
            w1 = R(xp1s, 0, 128, dg * 128, 128)
            w2 = R(xp2s, 0, 128, dg * 48, 48)
            st0, sp = dg == 0, dg == NDG - 1
            A.op("PE", lambda te, ubr=ubr, w1=w1, st0=st0, sp=sp: te.matmul(
                P6[:], w1.ap, ubr.ap, start=st0, stop=sp),
                reads=[ubr, w1], writes=[R(P6)])
            A.op("PE", lambda te, ubr=ubr, w2=w2, st0=st0, sp=sp: te.matmul(
                P7[0:48, :], w2.ap, ubr.ap, start=st0, stop=sp),
                reads=[ubr, w2], writes=[R(P7, 0, 48, 0, TT)])

        A.op("DVE", lambda v: v.tensor_copy(dtbf[:], P6[:]),
             reads=[R(P6)], writes=[R(dtbf)])
        A.op("ACT", lambda sc: sc.copy(bctm[:], P7[0:48, :]),
             reads=[R(P7, 0, 48, 0, TT)], writes=[R(bctm)])
        A.op("DVE", lambda v: v.tensor_copy(bctb[:], bctm[:]),
             reads=[R(bctm)], writes=[R(bctb)])
        A.dma("SP", dt_o[:, xlo:xlo + TT], R(dtbf))
        A.dma("SP", bc_o[:, xlo:xlo + TT], R(bctm))

        for n in range(NST):
            for (boff, rep) in ((0, brep), (16, crep)):
                pb = P3 if ((n + boff) % 2 == 0) else P4
                sel = R(selbcs, 0, 48, (boff + n) * 128, 128)
                rrg = R(rep, 0, 128, n * TT, TT)
                A.op("PE", lambda te, sel=sel, pb=pb: te.matmul(
                    pb[:], sel.ap, bctb[:], start=True, stop=True),
                    reads=[sel, R(bctb)], writes=[R(pb)])
                A.op("ACT", lambda sc, rrg=rrg, pb=pb: sc.copy(rrg.ap, pb[:]),
                     reads=[R(pb)], writes=[rrg])

        for dg in range(NDG):
            ur = R(ub, 0, 128, dg * TT, TT)
            wdt = R(dtps, 0, 128, dg * 128, 128)
            A.op("PE", lambda te, wdt=wdt: te.matmul(P6[:], wdt.ap, dtbf[:],
                 start=True, stop=True), reads=[wdt, R(dtbf)], writes=[R(P6)])
            bcol = R(dtbs, 0, 128, dg, 1)
            A.op("DVE", lambda v, bcol=bcol: v.tensor_scalar(deld[:], P6[:], bcol.ap,
                 None, ALU.add), reads=[R(P6)], writes=[R(deld)], sreads=[bcol])
            A.op("DVE", lambda v: v.tensor_scalar_mul(sp1[:], deld[:], -1.0),
                 reads=[R(deld)], writes=[R(sp1)])
            A.op("DVE", lambda v: v.tensor_max(sp2[:], sp1[:], deld[:]),
                 reads=[R(sp1), R(deld)], writes=[R(sp2)])
            A.op("ACT", lambda sc: sc.activation(sp1[:], sp2[:], AF.Exp, scale=-1.0),
                 reads=[R(sp2)], writes=[R(sp1)])
            A.op("ACT", lambda sc: sc.activation(sp1[:], sp1[:], AF.Ln, bias=1.0),
                 reads=[R(sp1)], writes=[R(sp1)])
            A.op("DVE", lambda v: v.tensor_scalar_max(sp2[:], deld[:], 0.0),
                 reads=[R(deld)], writes=[R(sp2)])
            A.op("DVE", lambda v: v.tensor_add(deld[:], sp1[:], sp2[:]),
                 reads=[R(sp1), R(sp2)], writes=[R(deld)])
            A.op("DVE", lambda v: v.reduce_sum(sd1[:], deld[:], axis=AX),
                 reads=[R(deld)], writes=[R(sd1)])
            sdc = R(sumd, 0, 128, dg, 1)
            A.op("DVE", lambda v, sdc=sdc: v.tensor_add(sdc.ap, sdc.ap, sd1[:]),
                 reads=[R(sd1), sdc], writes=[sdc])
            A.op("DVE", lambda v, ur=ur: v.tensor_mul(dud[:], deld[:], ur.ap),
                 reads=[R(deld), ur], writes=[R(dud)])
            for n in range(NST):
                col = dg * NST + n
                pp = n % 2
                da = R(dab, 0, 128, pp * TT, TT)
                db = R(dbub, 0, 128, pp * TT, TT)
                hh = R(hb, 0, 128, pp * TT, TT)
                yb = R(ybb, 0, 128, pp * TT, TT)
                acol = R(amat, 0, 128, col, 1)
                scol = R(stv, 0, 128, col, 1)
                brn = R(brep, 0, 128, n * TT, TT)
                crn = R(crep, 0, 128, n * TT, TT)
                A.op("ACT", lambda sc, da=da, acol=acol: sc.activation(
                    da.ap, deld[:], AF.Exp, scale=acol.ap),
                    reads=[R(deld)], writes=[da], sreads=[acol])
                A.op("POOL", lambda g, db=db, brn=brn: g.tensor_mul(db.ap, dud[:], brn.ap),
                     reads=[R(dud), brn], writes=[db])
                A.op("DVE", lambda v, hh=hh, da=da, db=db, scol=scol: v.tensor_tensor_scan(
                    hh.ap, da.ap, db.ap, scol.ap, ALU.mult, ALU.add),
                    reads=[da, db], writes=[hh], sreads=[scol])
                A.op("POOL", lambda g, scol=scol, hh=hh: g.tensor_copy(
                    scol.ap, hh.t[0:128, hh.c0 + TT - 1:hh.c0 + TT]),
                    reads=[hh], writes=[scol])
                A.op("POOL", lambda g, yb=yb, hh=hh, crn=crn: g.tensor_mul(
                    yb.ap, hh.ap, crn.ap), reads=[hh, crn], writes=[yb])
                if n == 0:
                    A.op("POOL", lambda g, yb=yb: g.tensor_copy(yac[:], yb.ap),
                         reads=[yb], writes=[R(yac)])
                else:
                    A.op("POOL", lambda g, yb=yb: g.tensor_add(yac[:], yac[:], yb.ap),
                         reads=[yb, R(yac)], writes=[R(yac)])
            xt = R(x1t, 0, 128, (dg % 2) * TT, TT)
            dcol = R(dsks, 0, 128, dg, 1)
            A.op("DVE", lambda v, xt=xt, ur=ur, dcol=dcol: v.scalar_tensor_tensor(
                xt.ap, ur.ap, dcol.ap, yac[:], ALU.mult, ALU.add),
                reads=[ur, R(yac)], writes=[xt], sreads=[dcol])
            xr = R(xbuf, 0, 128, dg * 515 + 3, 512)
            A.op("DVE", lambda v, xt=xt, xr=xr: v.tensor_add(xt.ap, xt.ap, xr.ap),
                 reads=[xt, xr], writes=[xt])
            A.dma("SP", x1p_o[dg * 128:(dg + 1) * 128, xlo:xlo + TT], xt)

    for dg in range(NDG):
        er = R(elb, 0, 128, dg * NST, NST)
        ar = R(amat, 0, 128, dg * NST, NST)
        sdc = R(sumd, 0, 128, dg, 1)
        A.op("ACT", lambda sc, er=er, ar=ar, sdc=sdc: sc.activation(
            er.ap, ar.ap, AF.Exp, scale=sdc.ap),
            reads=[ar], writes=[er], sreads=[sdc])
    A.dma("SP", el_o[:, 0:NDG * NST], R(elb))
    A.dma("SP", el_o[:, NDG * NST:2 * NDG * NST], R(stv))
    A.fence()
    return A


def stage_launch1(inputs):
    x = np.ascontiguousarray(inputs["L0_attn_x"][0])
    conv_w = inputs["L0_attn_conv_w"][:, 0, :]
    conv_b = inputs["L0_attn_conv_b"]
    xproj = inputs["L0_attn_xproj_w"]
    dtw = inputs["L0_attn_dtproj_w"]
    dtb = inputs["L0_attn_dtproj_b"]
    alog = inputs["L0_attn_A_log"]
    dsk = inputs["L0_attn_D_skip"]

    cw_h = np.ascontiguousarray(conv_w.reshape(NDG, 128, 4).transpose(1, 0, 2)
                                .reshape(128, NDG * 4))
    cb_h = np.ascontiguousarray(conv_b.reshape(NDG, 128).T)
    xp1_h = np.ascontiguousarray(xproj[:128, :].T)
    xp2_h = np.zeros((D, 48), np.float32)
    xp2_h[:, 0:16] = xproj[128:144, :].T
    xp2_h[:, 32:48] = xproj[144:160, :].T
    dtp_h = np.ascontiguousarray(dtw.T)
    dtb_h = np.ascontiguousarray(dtb.reshape(NDG, 128).T)
    alog_h = np.ascontiguousarray(alog.reshape(NDG, 128, NST).transpose(1, 0, 2)
                                  .reshape(128, NDG * NST))
    dsk_h = np.ascontiguousarray(dsk.reshape(NDG, 128).T)
    onesc = np.ones((128, 1), np.float32)
    onesr = np.ones((1, 128), np.float32)
    selbc = np.zeros((48, 32 * 128), BF)
    for n in range(16):
        selbc[n, n * 128:(n + 1) * 128] = 1
        selbc[32 + n, (16 + n) * 128:(17 + n) * 128] = 1

    maps = []
    for c in range(NCORES):
        t0 = c * TC
        xs = np.zeros((TC + 3, D), np.float32)
        lo = max(t0 - 3, 0)
        xs[3 - (t0 - lo):] = x[lo:t0 + TC]
        maps.append(dict(
            xT=np.ascontiguousarray(xs.T), cw=cw_h, cb=cb_h, xp1=xp1_h,
            xp2=xp2_h, dtp=dtp_h, dtb=dtb_h, alog=alog_h, dsk=dsk_h,
            onesc=onesc, onesr=onesr, selbc=selbc))
    return maps


def build_launch2(dbg=False):
    A = Auto("l2")
    TT = 512
    NT = TC // TT
    NIC = 11
    NICS = 44

    x1p = A.inp("x1p", [D, TC])
    dtin = A.inp("dtin", [128, TC], BF16)
    bcin = A.inp("bcin", [48, TC])
    elall = A.inp("elall", [128, 7 * 512])
    dtp = A.inp("dtp", [128, D])
    dtb = A.inp("dtb", [128, NDG])
    alog = A.inp("alog", [128, NDG * NST])
    selc = A.inp("selc", [48, NST * 128], BF16)
    sel8 = A.inp("sel8", [8, 8 * 128], BF16)
    onesr = A.inp("onesr", [1, 128])
    onesc = A.inp("onesc", [128, 1])
    idf = A.inp("idf", [128, 128])
    rw = A.inp("rw", [D, 8])
    gup = A.inp("gup", [128, 176 * 2048])
    dwn = A.inp("dwn", [128, 8 * 16 * 1408])
    shg = A.inp("shg", [128, 44 * 2048])
    shu = A.inp("shu", [128, 44 * 2048])
    shd = A.inp("shd", [128, 4 * 16 * 1408])
    sgw = A.inp("sgw", [128, NDG])

    x2_o = A.out("x2", [D, TC])
    if dbg:
        dbg_x1 = A.out("dbg_x1", [D, TC])
        dbg_ln = A.out("dbg_ln", [D, TC], BF16)
        dbg_wq = A.out("dbg_wq", [128, 4 * 4 * 8])
        dbg_sinit = A.out("dbg_sinit", [128, 256])
        dbg_aw = A.out("dbg_aw", [128, NIC * 512], BF16)
        dbg_lgt = A.out("dbg_lgt", [128, 4 * 8])
        dbg_ex = A.out("dbg_ex", [128, 4 * 8])
        dbg_ge = A.out("dbg_ge", [128, 4 * 8])
        dbg_m1 = A.out("dbg_m1", [128, 8])
        dbg_oacc = A.out("dbg_oacc", [128, NDG * 512])
        dbg_aws = A.out("dbg_aws", [128, NIC * 512], BF16)

    dtps = A.sbuf("dtps", [128, D], BF16)
    dtbs = A.sbuf("dtbs", [128, NDG], F32)
    alogs = A.sbuf("alogs", [128, NDG * NST], F32)
    amat = A.sbuf("amat", [128, NDG * NST], F32)
    selcs = A.sbuf("selcs", [48, NST * 128], BF16)
    sel8s = A.sbuf("sel8s", [8, 8 * 128], BF16)
    onesrs = A.sbuf("onesrs", [1, 128], F32)
    onescs_l2 = A.sbuf("onescs_l2", [128, 1], F32)
    idfs = A.sbuf("idfs", [128, 128], F32)
    rws = A.sbuf("rws", [128, NDG * 8], BF16)
    sgws = A.sbuf("sgws", [128, NDG], BF16)
    sinit = A.sbuf("sinit", [128, NDG * NST], F32)
    sinit_t = A.sbuf("sinit_t", [128, 256], F32)
    cumst = A.sbuf("cumst", [128, NDG], F32)
    elalls = A.sbuf("elalls", [128, 2 * 512], F32)

    x1t = A.sbuf("x1t", [128, NDG * TT], F32)
    ln2b = A.sbuf("ln2b", [128, NDG * TT], BF16)
    dts = A.sbuf("dts", [128, TT], BF16)
    bctb = A.sbuf("bctb", [48, TT], BF16)
    crep = A.sbuf("crep", [128, NST * TT], BF16)
    deld = A.sbuf("deld", [128, TT], F32)
    sp1 = A.sbuf("sp1", [128, TT], F32)
    sp2 = A.sbuf("sp2", [128, TT], F32)
    cumd = A.sbuf("cumd", [128, TT], F32)
    pnb = A.sbuf("pnb", [128, 2 * TT], F32)
    corb = A.sbuf("corb", [128, 2 * TT], F32)
    cacc = A.sbuf("cacc", [128, TT], F32)
    srow = A.sbuf("srow", [1, 4 * TT], F32)
    sqb = A.sbuf("sqb", [128, 2 * TT], F32)
    lgt = A.sbuf("lgt", [128, 4 * 8], F32)
    m1b = A.sbuf("m1b", [128, 8], F32)
    geb = A.sbuf("geb", [128, 4 * 8], F32)
    exb = A.sbuf("exb", [128, 4 * 8], F32)
    wqb = A.sbuf("wqb", [128, 4 * 8], F32)
    wtm = A.sbuf("wtm", [8, TT], BF16)
    wrep = A.sbuf("wrep", [128, 8 * TT], BF16)
    sigr = A.sbuf("sigr", [128, TT], BF16)
    sgrow = A.sbuf("sgrow", [1, TT], F32)
    awb = A.sbuf("awb", [128, NIC * TT], BF16)
    oacc = A.sbuf("oacc", [128, NDG * TT], F32)
    gwb = A.sbuf("gwb", [128, 2 * NDG * 128], BF16)
    uwb = A.sbuf("uwb", [128, 2 * NDG * 128], BF16)
    dwb = A.sbuf("dwb", [128, 2 * NIC * 128], BF16)
    sgl = A.sbuf("sgl", [128, TT], F32)
    x2t = A.sbuf("x2t", [128, 2 * TT], F32)

    PS = [A.psum(f"PS{i}", [128, TT]) for i in range(8)]

    A.dma("SP", R(dtbs), dtb)
    A.dma("SP", R(alogs), alog)
    A.dma("SP", R(selcs), selc)
    A.dma("SP", R(sel8s), sel8)
    A.dma("SP", R(onesrs), onesr)
    A.dma("SP", R(onescs_l2), onesc)
    A.dma("SP", R(idfs), idf)
    A.dma("POOL", R(dtps), dtp)
    A.dma("POOL", rws[:, :].rearrange("p (g e) -> p g e", g=NDG),
          rw.rearrange("(g p) e -> p g e", p=128), writes=[R(rws)])
    A.dma("POOL", R(sgws), sgw)

    A.op("ACT", lambda sc: sc.activation(amat[:], alogs[:], AF.Exp),
         reads=[R(alogs)], writes=[R(amat)])
    A.op("DVE", lambda v: v.tensor_scalar_mul(amat[:], amat[:], -1.0),
         reads=[R(amat)], writes=[R(amat)])
    A.op("POOL", lambda g: g.memset(cumst[:], 0.0), writes=[R(cumst)])

    A.op("POOL", lambda g: g.memset(sinit[:], 0.0), writes=[R(sinit)])
    for j in range(7):
        hb = R(elalls, 0, 128, (j % 2) * 512, 512)
        A.dma("SP", hb, elall[:, j * 512:(j + 1) * 512])
        Ej = R(elalls, 0, 128, (j % 2) * 512, 256)
        Lj = R(elalls, 0, 128, (j % 2) * 512 + 256, 256)
        A.op("POOL", lambda g, Ej=Ej: g.tensor_mul(sinit_t[:], sinit[:], Ej.ap),
             reads=[R(sinit), Ej], writes=[R(sinit_t)])
        A.op("POOL", lambda g, Lj=Lj: g.tensor_add(sinit[:], sinit_t[:], Lj.ap),
             reads=[R(sinit_t), Lj], writes=[R(sinit)])

    MEAN, VAR, RSTD, MR = 0, TT, 2 * TT, 3 * TT

    def rr(off, n=TT):
        return R(srow, 0, 1, off, n)

    for j in range(NT):
        xlo = j * TT
        for dg in range(NDG):
            A.dma("SP", R(x1t, 0, 128, dg * TT, TT),
                  x1p[dg * 128:(dg + 1) * 128, xlo:xlo + TT])
        A.dma("SP", R(dts), dtin[:, xlo:xlo + TT])
        A.dma("POOL", R(bctb), bcin[:, xlo:xlo + TT])

        for n in range(NST):
            pb = PS[n % 2]
            sel = R(selcs, 0, 48, n * 128, 128)
            rg = R(crep, 0, 128, n * TT, TT)
            A.op("PE", lambda te, sel=sel, pb=pb: te.matmul(
                pb[:], sel.ap, bctb[:], start=True, stop=True),
                reads=[sel, R(bctb)], writes=[R(pb)])
            A.op("ACT", lambda sc, rg=rg, pb=pb: sc.copy(rg.ap, pb[:]),
                 reads=[R(pb)], writes=[rg])

        for dg in range(NDG):
            wdt = R(dtps, 0, 128, dg * 128, 128)
            A.op("PE", lambda te, wdt=wdt: te.matmul(PS[2][:], wdt.ap, dts[:],
                 start=True, stop=True), reads=[wdt, R(dts)], writes=[R(PS[2])])
            bcol = R(dtbs, 0, 128, dg, 1)
            A.op("DVE", lambda v, bcol=bcol: v.tensor_scalar(deld[:], PS[2][:],
                 bcol.ap, None, ALU.add), reads=[R(PS[2])], writes=[R(deld)],
                 sreads=[bcol])
            A.op("DVE", lambda v: v.tensor_scalar_mul(sp1[:], deld[:], -1.0),
                 reads=[R(deld)], writes=[R(sp1)])
            A.op("DVE", lambda v: v.tensor_max(sp2[:], sp1[:], deld[:]),
                 reads=[R(sp1), R(deld)], writes=[R(sp2)])
            A.op("ACT", lambda sc: sc.activation(sp1[:], sp2[:], AF.Exp, scale=-1.0),
                 reads=[R(sp2)], writes=[R(sp1)])
            A.op("ACT", lambda sc: sc.activation(sp1[:], sp1[:], AF.Ln, bias=1.0),
                 reads=[R(sp1)], writes=[R(sp1)])
            A.op("DVE", lambda v: v.tensor_scalar_max(sp2[:], deld[:], 0.0),
                 reads=[R(deld)], writes=[R(sp2)])
            A.op("DVE", lambda v: v.tensor_add(deld[:], sp1[:], sp2[:]),
                 reads=[R(sp1), R(sp2)], writes=[R(deld)])
            scol = R(cumst, 0, 128, dg, 1)
            A.op("DVE", lambda v, scol=scol: v.tensor_tensor_scan(
                cumd[:], deld[:], deld[:], scol.ap, ALU.add, ALU.bypass),
                reads=[R(deld)], writes=[R(cumd)], sreads=[scol])
            A.op("POOL", lambda g, scol=scol: g.tensor_copy(
                scol.ap, cumd[:, TT - 1:TT]), reads=[R(cumd)], writes=[scol])
            for n in range(NST):
                col = dg * NST + n
                pp = n % 2
                pn = R(pnb, 0, 128, pp * TT, TT)
                co = R(corb, 0, 128, pp * TT, TT)
                acol = R(amat, 0, 128, col, 1)
                sco = R(sinit, 0, 128, col, 1)
                crn = R(crep, 0, 128, n * TT, TT)
                A.op("ACT", lambda sc, pn=pn, acol=acol: sc.activation(
                    pn.ap, cumd[:], AF.Exp, scale=acol.ap),
                    reads=[R(cumd)], writes=[pn], sreads=[acol])
                A.op("DVE", lambda v, co=co, pn=pn, sco=sco, crn=crn:
                     v.scalar_tensor_tensor(co.ap, pn.ap, sco.ap, crn.ap,
                     ALU.mult, ALU.mult), reads=[pn, crn], writes=[co], sreads=[sco])
                if n == 0:
                    A.op("POOL", lambda g, co=co: g.tensor_copy(cacc[:], co.ap),
                         reads=[co], writes=[R(cacc)])
                else:
                    A.op("POOL", lambda g, co=co: g.tensor_add(cacc[:], cacc[:], co.ap),
                         reads=[co, R(cacc)], writes=[R(cacc)])
            xr = R(x1t, 0, 128, dg * TT, TT)
            A.op("DVE", lambda v, xr=xr: v.tensor_add(xr.ap, xr.ap, cacc[:]),
                 reads=[xr, R(cacc)], writes=[xr])

        if dbg:
            for dg in range(NDG):
                A.dma("ACT", dbg_x1[dg * 128:(dg + 1) * 128, xlo:xlo + TT],
                      R(x1t, 0, 128, dg * TT, TT))
            if j == 0:
                A.dma("ACT", dbg_sinit, R(sinit))
        for dg in range(NDG):
            xr = R(x1t, 0, 128, dg * TT, TT)
            sq = R(sqb, 0, 128, (dg % 2) * TT, TT)
            A.op("ACT", lambda sc, xr=xr, sq=sq: sc.activation(sq.ap, xr.ap, AF.Square),
                 reads=[xr], writes=[sq])
            st0, sp = dg == 0, dg == NDG - 1
            A.op("PE", lambda te, xr=xr, st0=st0, sp=sp: te.matmul(
                PS[3][0:1, :], onescs_l2[:], xr.ap, start=st0, stop=sp),
                reads=[xr, R(onescs_l2)], writes=[R(PS[3], 0, 1, 0, TT)])
            A.op("PE", lambda te, sq=sq, st0=st0, sp=sp: te.matmul(
                PS[4][0:1, :], onescs_l2[:], sq.ap, start=st0, stop=sp),
                reads=[sq, R(onescs_l2)], writes=[R(PS[4], 0, 1, 0, TT)])
        inv_d = 1.0 / D
        A.op("DVE", lambda v: v.tensor_scalar_mul(rr(MEAN).ap, PS[3][0:1, :], inv_d),
             reads=[R(PS[3], 0, 1, 0, TT)], writes=[rr(MEAN)])
        A.op("DVE", lambda v: v.tensor_mul(rr(VAR).ap, rr(MEAN).ap, rr(MEAN).ap),
             reads=[rr(MEAN)], writes=[rr(VAR)])
        A.op("DVE", lambda v: v.scalar_tensor_tensor(rr(VAR).ap, PS[4][0:1, :],
             inv_d, rr(VAR).ap, ALU.mult, ALU.subtract),
             reads=[R(PS[4], 0, 1, 0, TT), rr(VAR)], writes=[rr(VAR)])
        A.op("DVE", lambda v: v.tensor_scalar_add(rr(VAR).ap, rr(VAR).ap, 1e-5),
             reads=[rr(VAR)], writes=[rr(VAR)])
        A.op("ACT", lambda sc: sc.activation(rr(RSTD).ap, rr(VAR).ap, AF.Ln),
             reads=[rr(VAR)], writes=[rr(RSTD)])
        A.op("ACT", lambda sc: sc.activation(rr(RSTD).ap, rr(RSTD).ap, AF.Exp,
             scale=-0.5), reads=[rr(RSTD)], writes=[rr(RSTD)])
        A.op("DVE", lambda v: v.tensor_mul(rr(MR).ap, rr(MEAN).ap, rr(RSTD).ap),
             reads=[rr(MEAN), rr(RSTD)], writes=[rr(MR)])
        A.op("PE", lambda te: te.matmul(PS[3][:], onesrs[:], rr(RSTD).ap,
             start=True, stop=True), reads=[rr(RSTD), R(onesrs)], writes=[R(PS[3])])
        A.op("PE", lambda te: te.matmul(PS[4][:], onesrs[:], rr(MR).ap,
             start=True, stop=True), reads=[rr(MR), R(onesrs)], writes=[R(PS[4])])
        for dg in range(NDG):
            xr = R(x1t, 0, 128, dg * TT, TT)
            lr = R(ln2b, 0, 128, dg * TT, TT)
            sq = R(sqb, 0, 128, (dg % 2) * TT, TT)
            A.op("DVE", lambda v, xr=xr, sq=sq: v.tensor_mul(sq.ap, xr.ap, PS[3][:]),
                 reads=[xr, R(PS[3])], writes=[sq])
            A.op("DVE", lambda v, sq=sq, lr=lr: v.tensor_sub(lr.ap, sq.ap, PS[4][:]),
                 reads=[sq, R(PS[4])], writes=[lr])

        if dbg:
            for dg in range(NDG):
                A.dma("ACT", dbg_ln[dg * 128:(dg + 1) * 128, xlo:xlo + TT],
                      R(ln2b, 0, 128, dg * TT, TT))
        for tc4 in range(4):
            lg = R(lgt, 0, 128, tc4 * 8, 8)
            for kc in range(NDG):
                xst = R(ln2b, 0, 128, kc * TT + tc4 * 128, 128)
                wst = R(rws, 0, 128, kc * 8, 8)
                A.op("PE", lambda te, xst=xst, wst=wst, st0=kc == 0, sp=kc == NDG - 1:
                     te.matmul(PS[5][0:128, 0:8], xst.ap, wst.ap, start=st0, stop=sp),
                     reads=[xst, wst], writes=[R(PS[5], 0, 128, 0, 8)])
            A.op("DVE", lambda v, lg=lg: v.tensor_copy(lg.ap, PS[5][0:128, 0:8]),
                 reads=[R(PS[5], 0, 128, 0, 8)], writes=[lg])
            m1 = R(m1b, 0, 128, 0, 1)
            l2m = R(m1b, 0, 128, 1, 1)
            m1n = R(m1b, 0, 128, 2, 1)
            rde = R(m1b, 0, 128, 3, 1)
            rdc = R(m1b, 0, 128, 4, 1)
            m1c = R(m1b, 0, 128, 5, 1)
            l2c = R(m1b, 0, 128, 6, 1)
            ge = R(geb, 0, 128, tc4 * 8, 8)
            ex = R(exb, 0, 128, tc4 * 8, 8)
            wq = R(wqb, 0, 128, tc4 * 8, 8)
            A.op("DVE", lambda v, lg=lg, m1=m1: v.reduce_max(m1.ap, lg.ap, axis=AX),
                 reads=[lg], writes=[m1])
            A.op("POOL", lambda g, m1=m1, m1n=m1n: g.tensor_scalar_mul(
                m1n.ap, m1.ap, -1.0), reads=[m1], writes=[m1n])
            A.op("POOL", lambda g, m1=m1, m1c=m1c: g.tensor_copy(m1c.ap, m1.ap),
                 reads=[m1], writes=[m1c])
            A.op("DVE", lambda v, lg=lg, ge=ge, m1c=m1c: v.tensor_scalar(
                ge.ap, lg.ap, m1c.ap, None, ALU.is_ge), reads=[lg], writes=[ge],
                sreads=[m1c])
            A.op("DVE", lambda v, lg=lg, ge=ge: v.scalar_tensor_tensor(
                ge.ap, ge.ap, -1e30, lg.ap, ALU.mult, ALU.add),
                reads=[ge, lg], writes=[ge])
            A.op("DVE", lambda v, ge=ge, l2m=l2m: v.reduce_max(l2m.ap, ge.ap, axis=AX),
                 reads=[ge], writes=[l2m])
            A.op("POOL", lambda g, l2m=l2m, l2c=l2c: g.tensor_copy(l2c.ap, l2m.ap),
                 reads=[l2m], writes=[l2c])
            A.op("ACT", lambda sc, ex=ex, lg=lg, m1n=m1n: sc.activation(
                ex.ap, lg.ap, AF.Exp, bias=m1n.ap), reads=[lg], writes=[ex],
                sreads=[m1n])
            A.op("DVE", lambda v, ge=ge, lg=lg, l2c=l2c: v.tensor_scalar(
                ge.ap, lg.ap, l2c.ap, None, ALU.is_ge), reads=[lg], writes=[ge],
                sreads=[l2c])
            A.op("DVE", lambda v, ge=ge, ex=ex: v.tensor_mul(ex.ap, ex.ap, ge.ap),
                 reads=[ge, ex], writes=[ex])
            A.op("DVE", lambda v, ex=ex, rde=rde: v.reduce_sum(rde.ap, ex.ap, axis=AX),
                 reads=[ex], writes=[rde])
            A.op("DVE", lambda v, rde=rde: v.reciprocal(rde.ap, rde.ap),
                 reads=[rde], writes=[rde])
            A.op("POOL", lambda g, rde=rde, rdc=rdc: g.tensor_copy(rdc.ap, rde.ap),
                 reads=[rde], writes=[rdc])
            A.op("DVE", lambda v, wq=wq, ex=ex, rdc=rdc: v.tensor_scalar(
                wq.ap, ex.ap, rdc.ap, None, ALU.mult), reads=[ex], writes=[wq],
                sreads=[rdc])
            A.op("PE", lambda te, wq=wq: te.transpose(
                PS[5][0:8, 0:128], wq.ap, idfs[:]), reads=[wq, R(idfs)],
                writes=[R(PS[5], 0, 8, 0, 128)])
            A.op("ACT", lambda sc, tc4=tc4: sc.copy(wtm[0:8, tc4 * 128:(tc4 + 1) * 128],
                 PS[5][0:8, 0:128]), reads=[R(PS[5], 0, 8, 0, 128)],
                 writes=[R(wtm, 0, 8, tc4 * 128, 128)])
        for e in range(8):
            pb = PS[(e % 2) + 5]
            sel = R(sel8s, 0, 8, e * 128, 128)
            rg = R(wrep, 0, 128, e * TT, TT)
            A.op("PE", lambda te, sel=sel, pb=pb: te.matmul(pb[:], sel.ap, wtm[:],
                 start=True, stop=True), reads=[sel, R(wtm)], writes=[R(pb)])
            A.op("ACT", lambda sc, rg=rg, pb=pb: sc.copy(rg.ap, pb[:]),
                 reads=[R(pb)], writes=[rg])
        for kc in range(NDG):
            wst = R(sgws, 0, 128, kc, 1)
            xst = R(ln2b, 0, 128, kc * TT, TT)
            A.op("PE", lambda te, xst=xst, wst=wst, st0=kc == 0, sp=kc == NDG - 1:
                 te.matmul(PS[7][0:1, :], wst.ap, xst.ap, start=st0, stop=sp),
                 reads=[xst, wst], writes=[R(PS[7], 0, 1, 0, TT)])
        A.op("ACT", lambda sc: sc.activation(sgrow[:], PS[7][0:1, :], AF.Sigmoid),
             reads=[R(PS[7], 0, 1, 0, TT)], writes=[R(sgrow)])
        A.op("PE", lambda te: te.matmul(PS[7][:], onesrs[:], sgrow[:],
             start=True, stop=True), reads=[R(sgrow), R(onesrs)], writes=[R(PS[7])])
        A.op("ACT", lambda sc: sc.copy(sigr[:], PS[7][:]),
             reads=[R(PS[7])], writes=[R(sigr)])

        if dbg:
            A.dma("ACT", dbg_wq[:, j * 32:(j + 1) * 32], R(wqb))
            if j == 0:
                A.dma("ACT", dbg_lgt, R(lgt))
                A.dma("ACT", dbg_ex, R(exb))
                A.dma("ACT", dbg_ge, R(geb))
                A.dma("ACT", dbg_m1, R(m1b))
        vexp = [("m", e) for e in range(8)] + [("s", g) for g in range(4)]
        first = True
        for kind, e in vexp:
            nic = NIC if kind == "m" else 11
            for ic in range(nic):
                gb = R(gwb, 0, 128, (ic % 2) * NDG * 128, NDG * 128)
                ub2 = R(uwb, 0, 128, (ic % 2) * NDG * 128, NDG * 128)
                if kind == "m":
                    cig = e * 22 + ic
                    ciu = e * 22 + NIC + ic
                    A.dma("POOL", gb, gup[:, cig * 2048:(cig + 1) * 2048])
                    A.dma("POOL", ub2, gup[:, ciu * 2048:(ciu + 1) * 2048])
                else:
                    cis = e * 11 + ic
                    A.dma("POOL", gb, shg[:, cis * 2048:(cis + 1) * 2048])
                    A.dma("POOL", ub2, shu[:, cis * 2048:(cis + 1) * 2048])
                for kc in range(NDG):
                    xst = R(ln2b, 0, 128, kc * TT, TT)
                    A.op("PE", lambda te, gb=gb, kc=kc, xst=xst, st0=kc == 0,
                         sp=kc == NDG - 1: te.matmul(PS[0][:],
                         gb.t[0:128, gb.c0 + kc * 128:gb.c0 + (kc + 1) * 128],
                         xst.ap, start=st0, stop=sp), reads=[gb, xst], writes=[R(PS[0])])
                    A.op("PE", lambda te, ub2=ub2, kc=kc, xst=xst, st0=kc == 0,
                         sp=kc == NDG - 1: te.matmul(PS[1][:],
                         ub2.t[0:128, ub2.c0 + kc * 128:ub2.c0 + (kc + 1) * 128],
                         xst.ap, start=st0, stop=sp), reads=[ub2, xst], writes=[R(PS[1])])
                A.op("ACT", lambda sc: sc.activation(sgl[:], PS[0][:], AF.Silu),
                     reads=[R(PS[0])], writes=[R(sgl)])
                awr = R(awb, 0, 128, ic * TT, TT)
                A.op("DVE", lambda v, awr=awr: v.tensor_mul(awr.ap, sgl[:], PS[1][:]),
                     reads=[R(sgl), R(PS[1])], writes=[awr])
                wr = R(sigr) if kind == "s" else R(wrep, 0, 128, e * TT, TT)
                A.op("POOL", lambda g, awr=awr, wr=wr: g.tensor_mul(awr.ap, awr.ap,
                     wr.ap), reads=[awr, wr], writes=[awr])
            if dbg and j == 0 and kind == "m" and e == 0:
                A.dma("ACT", dbg_aw, R(awb))
            if dbg and j == 0 and kind == "s" and e == 0:
                A.dma("ACT", dbg_aws, R(awb))
            for dg in range(NDG):
                db = R(dwb, 0, 128, (dg % 2) * NIC * 128, nic * 128)
                base = (e * NDG + dg) * nic * 128
                if kind == "m":
                    A.dma("POOL", db, dwn[:, base:base + nic * 128])
                else:
                    A.dma("POOL", db, shd[:, base:base + nic * 128])
                psd = PS[2 + (dg % 2)]
                for ic in range(nic):
                    awr = R(awb, 0, 128, ic * TT, TT)
                    A.op("PE", lambda te, db=db, ic=ic, awr=awr, psd=psd,
                         st0=ic == 0, sp=ic == nic - 1: te.matmul(psd[:],
                         db.t[0:128, db.c0 + ic * 128:db.c0 + (ic + 1) * 128],
                         awr.ap, start=st0, stop=sp), reads=[db, awr],
                         writes=[R(psd)])
                orr = R(oacc, 0, 128, dg * TT, TT)
                if first:
                    A.op("DVE", lambda v, orr=orr, psd=psd: v.tensor_copy(
                        orr.ap, psd[:]), reads=[R(psd)], writes=[orr])
                else:
                    A.op("DVE", lambda v, orr=orr, psd=psd: v.tensor_add(
                        orr.ap, orr.ap, psd[:]), reads=[orr, R(psd)], writes=[orr])
            first = False

        if dbg and j == 0:
            A.dma("ACT", dbg_oacc, R(oacc))
        for dg in range(NDG):
            xr = R(x1t, 0, 128, dg * TT, TT)
            orr = R(oacc, 0, 128, dg * TT, TT)
            xo = R(x2t, 0, 128, (dg % 2) * TT, TT)
            A.op("DVE", lambda v, xr=xr, orr=orr, xo=xo: v.tensor_add(
                xo.ap, xr.ap, orr.ap), reads=[xr, orr], writes=[xo])
            A.dma("SP", x2_o[dg * 128:(dg + 1) * 128, xlo:xlo + TT], xo)

    A.fence()
    return A


def _cc_layout(w, ck=128):
    GP, N = w.shape
    G, NC = GP // 128, N // ck
    w4 = np.asarray(w).reshape(G, 128, NC, ck).transpose(1, 2, 0, 3)
    return np.ascontiguousarray(w4).reshape(128, NC * G * ck)


def _dn_layout(w, nblk, nic):
    NR, N = w.shape
    ndg = N // 128
    w5 = np.asarray(w).reshape(nblk, nic, 128, ndg, 128).transpose(2, 0, 3, 1, 4)
    return np.ascontiguousarray(w5).reshape(128, nblk * ndg * nic * 128)


def _act_chunks(a):
    GP, T = a.shape
    G, NC = GP // 128, T // 128
    a4 = np.asarray(a).reshape(G, 128, NC, 128).transpose(1, 2, 0, 3)
    return np.ascontiguousarray(a4).reshape(128, NC * G * 128)


def stage_launch2(inputs, l1_results):
    rw_h = np.ascontiguousarray(inputs["L0_ffn_router_weight"].T)
    gup_raw = inputs["L0_ffn_gate_up_proj"]
    gup_h = np.ascontiguousarray(np.transpose(gup_raw, (2, 0, 1))
                                 .reshape(D, 8 * 2816))
    dwn_raw = inputs["L0_ffn_down_proj"]
    dwn_h = np.ascontiguousarray(np.transpose(dwn_raw, (0, 2, 1))
                                 .reshape(8 * 1408, D))
    shg_h = np.ascontiguousarray(inputs["L0_ffn_sh_gate_w"].T)
    shu_h = np.ascontiguousarray(inputs["L0_ffn_sh_up_w"].T)
    shd_h = np.ascontiguousarray(inputs["L0_ffn_sh_down_w"].T)
    sgw_h = np.ascontiguousarray(
        inputs["L0_ffn_shared_gate_w"].reshape(NDG, 128).T)
    gup_c = _cc_layout(gup_h)
    dwn_c = _dn_layout(dwn_h, 8, 11)
    shg_c = _cc_layout(shg_h)
    shu_c = _cc_layout(shu_h)
    shd_c = _dn_layout(shd_h, 4, 11)
    dtw = inputs["L0_attn_dtproj_w"]
    dtp_h = np.ascontiguousarray(dtw.T)
    dtb_h = np.ascontiguousarray(inputs["L0_attn_dtproj_b"].reshape(NDG, 128).T)
    alog_h = np.ascontiguousarray(inputs["L0_attn_A_log"].reshape(NDG, 128, NST)
                                  .transpose(1, 0, 2).reshape(128, NDG * NST))
    selc = np.zeros((48, NST * 128), BF)
    for n in range(16):
        selc[32 + n, n * 128:(n + 1) * 128] = 1
    sel8 = np.zeros((8, 8 * 128), BF)
    for e in range(8):
        sel8[e, e * 128:(e + 1) * 128] = 1
    onesr = np.ones((1, 128), np.float32)
    onesc = np.ones((128, 1), np.float32)
    idf = np.eye(128, dtype=np.float32)

    maps = []
    for c in range(NCORES):
        el_hops = np.zeros((128, 7 * 512), np.float32)
        for j in range(7):
            if j < c:
                el_hops[:, j * 512:(j + 1) * 512] = l1_results[j]["el"]
            else:
                el_hops[:, j * 512:j * 512 + 256] = 1.0
        maps.append(dict(
            x1p=np.ascontiguousarray(l1_results[c]["x1p"]),
            dtin=np.ascontiguousarray(l1_results[c]["dt"]),
            bcin=np.ascontiguousarray(l1_results[c]["bc"]),
            elall=el_hops, dtp=dtp_h, dtb=dtb_h, alog=alog_h,
            selc=selc, sel8=sel8, onesr=onesr, onesc=onesc, idf=idf,
            rw=rw_h, gup=gup_c, dwn=dwn_c, shg=shg_c, shu=shu_c,
            shd=shd_c, sgw=sgw_h))
    return maps



NH = 32
DKP = 64
DVP = 128
CH = 128


class GdrBufs:

    def __init__(self, A):
        self.brel = A.sbuf("gd_brel", [128, CH], F32)
        self.btr = A.sbuf("gd_btr", [128, 128], F32)
        self.btn = A.sbuf("gd_btn", [128, NH], F32)
        self.lamcr = A.sbuf("gd_lamcr", [128, NH], F32)
        self.lamrow = A.sbuf("gd_lamrow", [1, NH], F32)
        self.dmb = A.sbuf("gd_dmb", [128, CH], F32)
        self.dmt = A.sbuf("gd_dmt", [128, CH], F32)
        self.abf = A.sbuf("gd_abf", [128, CH], BF16)
        self.atbf = A.sbuf("gd_atbf", [128, CH], BF16)
        self.qq = A.sbuf("gd_qq", [128, 64], BF16)
        self.qq2 = A.sbuf("gd_qq2", [128, 64], BF16)
        self.ttb = A.sbuf("gd_ttb", [128, 64], BF16)
        self.id64 = A.sbuf("gd_id64", [128, 64], BF16)
        self.vtm = A.sbuf("gd_vtm", [128, DVP], BF16)
        self.rhsu = A.sbuf("gd_rhsu", [128, DVP], F32)
        self.rhsub = A.sbuf("gd_rhsub", [128, DVP], BF16)
        self.u1b = A.sbuf("gd_u1b", [128, DVP], BF16)
        self.ub = A.sbuf("gd_ub", [128, DVP], BF16)
        self.tmpb = A.sbuf("gd_tmpb", [128, DVP], F32)
        self.ktm = A.sbuf("gd_ktm", [128, DKP], BF16)
        self.pcolc = A.sbuf("gd_pcolc", [NH, 1], F32)
        self.sbf = A.sbuf("gd_sbf", [128, DVP], BF16)


def emit_chunk_head(A, B, PS, PSB, idbs, ltriss, sth, ktr, vtr, h, extra=None):
    pb = (h % 2) * 64
    bTc = R(B.btr, 0, 128, h, 1)
    beTc = R(B.btn, 0, 128, h, 1)
    lTc = R(B.btr, 0, 128, 64 + h, 1)
    phc = R(B.btr, 0, 128, 96 + h, 1)
    bec = R(B.btr, 0, 128, 32 + h, 1)
    lamc = R(B.lamcr, pb, 64, h, 1)
    idp = R(idbs, pb, 64, pb, 64)
    tp = (pb, pb)

    A.op("PE", lambda te, ktr=ktr: te.matmul(PS[0][0:128, 0:CH], ktr.ap, ktr.ap,
         start=True, stop=True), reads=[ktr], writes=[R(PS[0], 0, 128, 0, CH)])
    sel = R(A._sel32s, 0, NH, h * 128, 128)
    A.op("PE", lambda te, sel=sel: te.matmul(PS[1][0:128, 0:CH], sel.ap,
         B.brel[0:NH, :], start=True, stop=True),
         reads=[sel, R(B.brel, 0, NH, 0, CH)], writes=[R(PS[1], 0, 128, 0, CH)])
    A.op("DVE", lambda v, bTc=bTc: v.tensor_scalar(B.dmb[:], PS[1][0:128, 0:CH],
         -1.0, bTc.ap, ALU.mult, ALU.add), reads=[R(PS[1], 0, 128, 0, CH)],
         writes=[R(B.dmb)], sreads=[bTc])
    A.op("DVE", lambda v: v.tensor_scalar_min(B.dmb[:], B.dmb[:], 0.0),
         reads=[R(B.dmb)], writes=[R(B.dmb)])
    A.op("ACT", lambda sc: sc.activation(B.dmb[:], B.dmb[:], AF.Exp),
         reads=[R(B.dmb)], writes=[R(B.dmb)])
    A.op("DVE", lambda v: v.tensor_mul(B.dmt[:], B.dmb[:], ltriss[:]),
         reads=[R(B.dmb), R(ltriss)], writes=[R(B.dmt)])
    A.op("DVE", lambda v, beTc=beTc: v.scalar_tensor_tensor(B.abf[:], B.dmt[:],
         beTc.ap, PS[0][0:128, 0:CH], ALU.mult, ALU.mult),
         reads=[R(B.dmt), R(PS[0], 0, 128, 0, CH)], writes=[R(B.abf)],
         sreads=[beTc])
    A.op("PE", lambda te: te.transpose(PSB[0:128, 0:CH], B.abf[:], idbs[:]),
         reads=[R(B.abf), R(idbs)], writes=[R(PSB, 0, 128, 0, CH)])
    A.op("ACT", lambda sc: sc.copy(B.atbf[:], PSB[0:128, 0:CH]),
         reads=[R(PSB, 0, 128, 0, CH)], writes=[R(B.atbf)])
    for b in range(2):
        p0 = b * 64
        qr = R(B.qq, p0, 64, 0, 64)
        qtr = R(B.qq2, p0, 64, 0, 64)
        ttr = R(B.ttb, p0, 64, 0, 64)
        idr = R(B.id64, p0, 64, 0, 64)
        ar = R(B.abf, p0, 64, p0, 64)
        atr = R(B.atbf, p0, 64, p0, 64)
        A.op("DVE", lambda v, qr=qr, ar=ar: v.tensor_copy(qr.ap, ar.ap),
             reads=[ar], writes=[qr])
        A.op("DVE", lambda v, qtr=qtr, atr=atr: v.tensor_copy(qtr.ap, atr.ap),
             reads=[atr], writes=[qtr])
        A.op("DVE", lambda v, ttr=ttr, idr=idr, qtr=qtr: v.tensor_add(
            ttr.ap, idr.ap, qtr.ap), reads=[idr, qtr], writes=[ttr])
        for lev in range(1, 6):
            psa = R(PS[3], p0, 64, 0, 64)
            psb = R(PS[6], p0, 64, 0, 64)
            tpb = (p0, p0)
            A.op("PE", lambda te, qr=qr, qtr=qtr, psa=psa, tpb=tpb: te.matmul(
                psa.ap, qtr.ap, qr.ap, start=True, stop=True, tile_position=tpb),
                reads=[qr, qtr], writes=[psa])
            A.op("PE", lambda te, qr=qr, qtr=qtr, psb=psb, tpb=tpb: te.matmul(
                psb.ap, qr.ap, qtr.ap, start=True, stop=True, tile_position=tpb),
                reads=[qr, qtr], writes=[psb])
            A.op("ACT", lambda sc, qr=qr, psa=psa: sc.copy(qr.ap, psa.ap),
                 reads=[psa], writes=[qr])
            A.op("ACT", lambda sc, qtr=qtr, psb=psb: sc.copy(qtr.ap, psb.ap),
                 reads=[psb], writes=[qtr])
            A.op("PE", lambda te, qr=qr, ttr=ttr, psa=psa, tpb=tpb: te.matmul(
                psa.ap, qr.ap, ttr.ap, start=True, stop=True, tile_position=tpb),
                reads=[qr, ttr], writes=[psa])
            A.op("DVE", lambda v, ttr=ttr, psa=psa: v.tensor_add(ttr.ap, ttr.ap,
                 psa.ap), reads=[ttr, psa], writes=[ttr])
    sbfr = R(B.sbf, pb, 64, 0, DVP)
    A.op("ACT", lambda sc, sbfr=sbfr, sth=sth: sc.copy(sbfr.ap, sth.ap),
         reads=[sth], writes=[sbfr])
    A.op("PE", lambda te, ktr=ktr, sbfr=sbfr: te.matmul(PS[0][0:128, 0:DVP],
         ktr.ap, sbfr.ap, start=True, stop=True), reads=[ktr, sbfr],
         writes=[R(PS[0], 0, 128, 0, DVP)])
    A.op("PE", lambda te, vtr=vtr: te.transpose(PSB[0:128, 128:128 + DVP], vtr.ap,
         idbs[:]), reads=[vtr, R(idbs)], writes=[R(PSB, 0, 128, 128, DVP)])
    A.op("ACT", lambda sc: sc.copy(B.vtm[:], PSB[0:128, 128:128 + DVP]),
         reads=[R(PSB, 0, 128, 128, DVP)], writes=[R(B.vtm)])
    A.op("DVE", lambda v, lTc=lTc: v.tensor_scalar(B.rhsu[:], PS[0][0:128, 0:DVP],
         lTc.ap, -1.0, ALU.mult, ALU.mult), reads=[R(PS[0], 0, 128, 0, DVP)],
         writes=[R(B.rhsu)], sreads=[lTc])
    A.op("DVE", lambda v: v.tensor_add(B.rhsu[:], B.rhsu[:], B.vtm[:]),
         reads=[R(B.rhsu), R(B.vtm)], writes=[R(B.rhsu)])
    A.op("DVE", lambda v, bec=bec: v.tensor_scalar(B.rhsub[:], B.rhsu[:], bec.ap,
         None, ALU.mult), reads=[R(B.rhsu)], writes=[R(B.rhsub)], sreads=[bec])
    A.op("PE", lambda te: te.matmul(PS[0][0:64, 0:DVP], B.ttb[0:64, :],
         B.rhsub[0:64, :], start=True, stop=True),
         reads=[R(B.ttb, 0, 64, 0, 64), R(B.rhsub, 0, 64, 0, DVP)],
         writes=[R(PS[0], 0, 64, 0, DVP)])
    A.op("ACT", lambda sc: sc.copy(B.u1b[0:64, :], PS[0][0:64, 0:DVP]),
         reads=[R(PS[0], 0, 64, 0, DVP)], writes=[R(B.u1b, 0, 64, 0, DVP)])
    A.op("PE", lambda te: te.matmul(PS[0][0:64, 0:DVP], B.atbf[0:64, 64:128],
         B.u1b[0:64, :], start=True, stop=True),
         reads=[R(B.atbf, 0, 64, 64, 64), R(B.u1b, 0, 64, 0, DVP)],
         writes=[R(PS[0], 0, 64, 0, DVP)])
    A.op("DVE", lambda v: v.tensor_add(B.tmpb[64:128, :], B.rhsub[64:128, :],
         PS[0][0:64, 0:DVP]), reads=[R(B.rhsub, 64, 64, 0, DVP),
         R(PS[0], 0, 64, 0, DVP)], writes=[R(B.tmpb, 64, 64, 0, DVP)])
    A.op("DVE", lambda v: v.tensor_copy(B.u1b[64:128, :], B.tmpb[64:128, :]),
         reads=[R(B.tmpb, 64, 64, 0, DVP)], writes=[R(B.u1b, 64, 64, 0, DVP)])
    A.op("PE", lambda te: te.matmul(PS[0][64:128, 0:DVP], B.ttb[64:128, :],
         B.u1b[64:128, :], start=True, stop=True, tile_position=(64, 64)),
         reads=[R(B.ttb, 64, 64, 0, 64), R(B.u1b, 64, 64, 0, DVP)],
         writes=[R(PS[0], 64, 64, 0, DVP)])
    A.op("DVE", lambda v: v.tensor_copy(B.ub[0:64, :], B.u1b[0:64, :]),
         reads=[R(B.u1b, 0, 64, 0, DVP)], writes=[R(B.ub, 0, 64, 0, DVP)])
    A.op("ACT", lambda sc: sc.copy(B.ub[64:128, :], PS[0][64:128, 0:DVP]),
         reads=[R(PS[0], 64, 64, 0, DVP)], writes=[R(B.ub, 64, 64, 0, DVP)])
    if extra is not None:
        extra["emit_output"](h, pb, bTc, lTc)
    A.op("PE", lambda te, ktr=ktr, idp=idp: te.transpose(PSB[0:128, 256:256 + DKP],
         ktr.ap, idp.ap), reads=[ktr, idp], writes=[R(PSB, 0, 128, 256, DKP)])
    A.op("DVE", lambda v, phc=phc: v.tensor_scalar(B.ktm[:], PSB[0:128, 256:256 + DKP],
         phc.ap, None, ALU.mult), reads=[R(PSB, 0, 128, 256, DKP)],
         writes=[R(B.ktm)], sreads=[phc])
    if pb == 0:
        A.op("PE", lambda te: te.matmul(PS[0][0:DKP, 0:DVP], B.ktm[:, 0:DKP],
             B.ub[:], start=True, stop=True), reads=[R(B.ktm), R(B.ub)],
             writes=[R(PS[0], 0, DKP, 0, DVP)])
        psd = R(PS[0], 0, DKP, 0, DVP)
    else:
        A.op("PE", lambda te: te.matmul(PS[0][64:128, 0:DVP], B.ktm[:, 0:DKP],
             B.ub[:], start=True, stop=True, tile_position=(0, 64)),
             reads=[R(B.ktm), R(B.ub)], writes=[R(PS[0], 64, 64, 0, DVP)])
        psd = R(PS[0], 64, 64, 0, DVP)
    A.op("DVE", lambda v, sth=sth, lamc=lamc, psd=psd: v.scalar_tensor_tensor(
        sth.ap, sth.ap, lamc.ap, psd.ap, ALU.mult, ALU.add),
        reads=[sth, psd], writes=[sth], sreads=[lamc])


def emit_chunk_prep(A, B, PS, idfs, onesrs, el128s, gcb, bet, c0, first):
    gcr = R(gcb, 0, NH, c0, CH)
    ber = R(bet, 0, NH, c0, CH)
    if first:
        A.op("DVE", lambda v, gcr=gcr: v.tensor_copy(B.brel[0:NH, :], gcr.ap),
             reads=[gcr], writes=[R(B.brel, 0, NH, 0, CH)])
    else:
        pcol = R(gcb, 0, NH, c0 - 1, 1)
        A.op("POOL", lambda g, pcol=pcol: g.tensor_copy(B.pcolc[:], pcol.ap),
             reads=[pcol], writes=[R(B.pcolc)])
        A.op("DVE", lambda v, gcr=gcr: v.tensor_scalar(B.brel[0:NH, :], gcr.ap,
             B.pcolc[:, 0:1], None, ALU.subtract), reads=[gcr],
             writes=[R(B.brel, 0, NH, 0, CH)], sreads=[R(B.pcolc)])
    A.op("POOL", lambda g, ber=ber: g.tensor_copy(B.brel[32:64, :], ber.ap),
         reads=[ber], writes=[R(B.brel, 32, 32, 0, CH)])
    A.op("ACT", lambda sc: sc.activation(B.brel[64:96, :], B.brel[0:NH, :],
         AF.Exp), reads=[R(B.brel, 0, NH, 0, CH)], writes=[R(B.brel, 64, 32, 0, CH)])
    bend = R(B.brel, 0, NH, CH - 1, 1)
    A.op("POOL", lambda g: g.tensor_copy(B.pcolc[:], B.brel[0:NH, CH - 1:CH]),
         reads=[R(B.brel, 0, NH, CH - 1, 1)], writes=[R(B.pcolc)])
    A.op("DVE", lambda v: v.tensor_scalar(B.brel[96:128, :], B.brel[0:NH, :],
         B.pcolc[:, 0:1], -1.0, ALU.subtract, ALU.mult),
         reads=[R(B.brel, 0, NH, 0, CH)], writes=[R(B.brel, 96, 32, 0, CH)],
         sreads=[R(B.pcolc)])
    A.op("ACT", lambda sc: sc.activation(B.brel[96:128, :], B.brel[96:128, :],
         AF.Exp), reads=[R(B.brel, 96, 32, 0, CH)], writes=[R(B.brel, 96, 32, 0, CH)])
    A.op("PE", lambda te: te.transpose(PS[5][0:128, 0:128], B.brel[:], idfs[:]),
         reads=[R(B.brel), R(idfs)], writes=[R(PS[5], 0, 128, 0, 128)])
    A.op("ACT", lambda sc: sc.copy(B.btr[:], PS[5][0:128, 0:128]),
         reads=[R(PS[5], 0, 128, 0, 128)], writes=[R(B.btr)])
    A.op("DVE", lambda v: v.tensor_scalar_mul(B.btn[:], B.btr[:, 32:64], -1.0),
         reads=[R(B.btr, 0, 128, 32, 32)], writes=[R(B.btn)])
    A.op("PE", lambda te: te.matmul(PS[5][0:1, 0:NH], el128s[:], B.btr[:, 64:96],
         start=True, stop=True), reads=[R(el128s), R(B.btr, 0, 128, 64, 32)],
         writes=[R(PS[5], 0, 1, 0, NH)])
    A.op("ACT", lambda sc: sc.copy(B.lamrow[:], PS[5][0:1, 0:NH]),
         reads=[R(PS[5], 0, 1, 0, NH)], writes=[R(B.lamrow)])
    A.op("PE", lambda te: te.matmul(PS[5][0:128, 0:NH], onesrs[:], B.lamrow[:],
         start=True, stop=True), reads=[R(B.lamrow), R(onesrs)],
         writes=[R(PS[5], 0, 128, 0, NH)])
    A.op("ACT", lambda sc: sc.copy(B.lamcr[:], PS[5][0:128, 0:NH]),
         reads=[R(PS[5], 0, 128, 0, NH)], writes=[R(B.lamcr)])


def build_launch3():
    A = Auto("l3")
    TT = 512
    NT = TC // TT
    NQG, NKG, NVG = 16, 16, 32

    x2T = A.inp("x2T", [D, TC + 3])
    rms1 = A.inp("rms1", [128, NDG])
    wq = A.inp("wq", [128, 16 * 2048])
    wk = A.inp("wk", [128, 16 * 2048])
    wv = A.inp("wv", [128, 32 * 2048])
    wg = A.inp("wg", [128, 32 * 2048])
    wa = A.inp("wa", [128, NDG * NH])
    wb = A.inp("wb", [128, NDG * NH])
    cwq = A.inp("cwq", [128, NQG * 4])
    cwk = A.inp("cwk", [128, NKG * 4])
    cwv = A.inp("cwv", [128, NVG * 4])
    dtb32 = A.inp("dtb32", [NH, 1])
    alog32 = A.inp("alog32", [NH, 1])
    onesc = A.inp("onesc", [128, 1])
    onesr = A.inp("onesr", [1, 128])
    hdsel = A.inp("hdsel", [128, 2])
    sel2 = A.inp("sel2", [2, 128])
    sel32 = A.inp("sel32", [NH, NH * 128])
    ltris = A.inp("ltris", [128, 128])
    idb = A.inp("idb", [128, 128], BF16)
    idf = A.inp("idf", [128, 128])
    el128 = A.inp("el128", [128, 1])

    qn_o = A.out("qn", [2048, TC], BF16)
    kn_o = A.out("kn", [2048, TC], BF16)
    vo_o = A.out("vo", [4096, TC], BF16)
    go_o = A.out("go", [4096, TC], BF16)
    gcum_o = A.out("gcum", [NH, TC])
    beta_o = A.out("beta", [NH, TC])
    glast_o = A.out("glast", [NH, 1])
    sl_o = A.out("sl", [128, (NH // 2) * DVP])

    rms1s = A.sbuf("rms1s", [128, NDG], F32)
    cwqs = A.sbuf("cwqs", [128, NQG * 4], F32)
    cwks = A.sbuf("cwks", [128, NKG * 4], F32)
    cwvs = A.sbuf("cwvs", [128, NVG * 4], F32)
    dtb32s = A.sbuf("dtb32s", [NH, 1], F32)
    alog32s = A.sbuf("alog32s", [NH, 1], F32)
    eAn = A.sbuf("eAn", [NH, 1], F32)
    onescs = A.sbuf("onescs", [128, 1], F32)
    onesrs = A.sbuf("onesrs", [1, 128], F32)
    hdsels = A.sbuf("hdsels", [128, 2], F32)
    sel2s = A.sbuf("sel2s", [2, 128], F32)
    sel32s = A.sbuf("sel32s", [NH, NH * 128], F32)
    A._sel32s = sel32s
    ltriss = A.sbuf("ltriss", [128, 128], F32)
    idbs = A.sbuf("idbs", [128, 128], BF16)
    idfs = A.sbuf("idfs", [128, 128], F32)
    el128s = A.sbuf("el128s", [128, 1], F32)
    x2b = A.sbuf("x2b", [128, NDG * 515], F32)
    hb3 = A.sbuf("hb3", [128, NDG * 515], BF16)
    sqb = A.sbuf("sqb", [128, 2 * 515], F32)
    srow = A.sbuf("srow", [1, 4 * 515], F32)
    wbuf = A.sbuf("wbuf", [128, 2 * NDG * 128], BF16)
    zb = A.sbuf("zb", [128, TT], F32)
    cvb = A.sbuf("cvb", [128, TT], F32)
    sqz = A.sbuf("sqz", [128, TT], F32)
    ztail = A.sbuf("ztail", [128, 64 * 3], F32)
    rsr = A.sbuf("rsr", [2, TT], F32)
    knb = A.sbuf("knb", [128, NKG * TT], BF16)
    vb3 = A.sbuf("vb3", [128, NVG * TT], BF16)
    gcb = A.sbuf("gcb", [NH, TC], F32)
    bet = A.sbuf("bet", [NH, TC], F32)
    zab = A.sbuf("zab", [NH, TT], F32)
    sp1 = A.sbuf("sp1", [NH, TT], F32)
    sp2 = A.sbuf("sp2", [NH, TT], F32)
    gcst = A.sbuf("gcst", [NH, 1], F32)
    qtb = A.sbuf("qtb", [128, TT], BF16)
    stv3 = A.sbuf("stv3", [128, (NH // 2) * DVP], F32)
    B = GdrBufs(A)
    PS = [A.psum(f"PS{i}", [128, TT]) for i in range(7)]
    PSB = A.psum("PSB", [128, 2 * TT], BF16)

    for t, s in ((rms1s, rms1), (cwqs, cwq), (cwks, cwk), (cwvs, cwv),
                 (dtb32s, dtb32), (alog32s, alog32), (onescs, onesc),
                 (onesrs, onesr), (hdsels, hdsel), (sel2s, sel2),
                 (ltriss, ltris), (idfs, idf), (el128s, el128)):
        A.dma("SP", R(t), s)
    A.dma("SP", R(sel32s), sel32)
    A.dma("SP", R(idbs), idb)
    A.op("ACT", lambda sc: sc.activation(eAn[:], alog32s[:], AF.Exp),
         reads=[R(alog32s)], writes=[R(eAn)])
    A.op("DVE", lambda v: v.tensor_scalar_mul(eAn[:], eAn[:], -1.0),
         reads=[R(eAn)], writes=[R(eAn)])
    A.op("POOL", lambda g: g.memset(gcst[:], 0.0), writes=[R(gcst)])
    A.op("POOL", lambda g: g.memset(stv3[:], 0.0), writes=[R(stv3)])
    A.op("POOL", lambda g: g.memset(ztail[:], 0.0), writes=[R(ztail)])
    A.op("DVE", lambda v: v.tensor_copy(B.id64[0:64, 0:64], idbs[0:64, 0:64]),
         reads=[R(idbs, 0, 64, 0, 64)], writes=[R(B.id64, 0, 64, 0, 64)])
    A.op("DVE", lambda v: v.tensor_copy(B.id64[64:128, 0:64], idbs[64:128, 64:128]),
         reads=[R(idbs, 64, 64, 64, 64)], writes=[R(B.id64, 64, 64, 0, 64)])

    MEAN, VAR, RSTD, MR = 0, 515, 2 * 515, 3 * 515

    def rr(off, c0=0, n=512):
        return R(srow, 0, 1, off + c0, n)

    def proj_psum(ps_reg, wsrc, fg, par, hc0, hw):
        wb_ = R(wbuf, 0, 128, par * NDG * 128, NDG * 128)
        A.dma("POOL", wb_, wsrc[:, fg * 2048:(fg + 1) * 2048])
        for kc in range(NDG):
            hr = R(hb3, 0, 128, kc * 515 + hc0, hw)
            A.op("PE", lambda te, wb_=wb_, kc=kc, hr=hr, ps_reg=ps_reg,
                 st0=kc == 0, sp=kc == NDG - 1: te.matmul(ps_reg.ap,
                 wb_.t[0:128, wb_.c0 + kc * 128:wb_.c0 + (kc + 1) * 128],
                 hr.ap, start=st0, stop=sp), reads=[wb_, hr], writes=[ps_reg])
        return wb_

    for j in range(NT):
        xlo = j * TT
        for dg in range(NDG):
            A.dma("SP", R(x2b, 0, 128, dg * 515, 515),
                  x2T[dg * 128:(dg + 1) * 128, xlo:xlo + 515])
        for dg in range(NDG):
            xr = R(x2b, 0, 128, dg * 515, 515)
            sq = R(sqb, 0, 128, (dg % 2) * 515, 515)
            A.op("ACT", lambda sc, xr=xr, sq=sq: sc.activation(sq.ap, xr.ap,
                 AF.Square), reads=[xr], writes=[sq])
            st0, sp = dg == 0, dg == NDG - 1
            A.op("PE", lambda te, xr=xr, st0=st0, sp=sp: te.matmul(
                PS[2][0:1, :], onescs[:], xr.t[0:128, xr.c0:xr.c0 + 512],
                start=st0, stop=sp), reads=[xr, R(onescs)],
                writes=[R(PS[2], 0, 1, 0, TT)])
            A.op("PE", lambda te, xr=xr, st0=st0, sp=sp: te.matmul(
                PS[4][0:1, 0:3], onescs[:], xr.t[0:128, xr.c0 + 512:xr.c0 + 515],
                start=st0, stop=sp), reads=[xr, R(onescs)],
                writes=[R(PS[4], 0, 1, 0, 3)])
            A.op("PE", lambda te, sq=sq, st0=st0, sp=sp: te.matmul(
                PS[3][0:1, :], onescs[:], sq.t[0:128, sq.c0:sq.c0 + 512],
                start=st0, stop=sp), reads=[sq, R(onescs)],
                writes=[R(PS[3], 0, 1, 0, TT)])
            A.op("PE", lambda te, sq=sq, st0=st0, sp=sp: te.matmul(
                PS[4][0:1, 4:7], onescs[:], sq.t[0:128, sq.c0 + 512:sq.c0 + 515],
                start=st0, stop=sp), reads=[sq, R(onescs)],
                writes=[R(PS[4], 0, 1, 4, 3)])
        inv_d = 1.0 / D
        for (coff, PSm, PSq, w) in ((0, R(PS[2], 0, 1, 0, TT), R(PS[3], 0, 1, 0, TT), 512),
                                    (512, R(PS[4], 0, 1, 0, 3), R(PS[4], 0, 1, 4, 3), 3)):
            A.op("DVE", lambda v, coff=coff, PSm=PSm, w=w: v.tensor_scalar_mul(
                rr(MEAN, coff, w).ap, PSm.ap, inv_d), reads=[PSm],
                writes=[rr(MEAN, coff, w)])
            A.op("DVE", lambda v, coff=coff, w=w: v.tensor_mul(
                rr(VAR, coff, w).ap, rr(MEAN, coff, w).ap, rr(MEAN, coff, w).ap),
                reads=[rr(MEAN, coff, w)], writes=[rr(VAR, coff, w)])
            A.op("DVE", lambda v, coff=coff, PSq=PSq, w=w: v.scalar_tensor_tensor(
                rr(VAR, coff, w).ap, PSq.ap, inv_d, rr(VAR, coff, w).ap,
                ALU.mult, ALU.subtract), reads=[PSq, rr(VAR, coff, w)],
                writes=[rr(VAR, coff, w)])
            A.op("DVE", lambda v, coff=coff, w=w: v.tensor_scalar_add(
                rr(MR, coff, w).ap, rr(VAR, coff, w).ap, 1e-5),
                reads=[rr(VAR, coff, w)], writes=[rr(MR, coff, w)])
            A.op("DVE", lambda v, coff=coff, w=w: v.reciprocal(
                rr(RSTD, coff, w).ap, rr(MR, coff, w).ap),
                reads=[rr(MR, coff, w)], writes=[rr(RSTD, coff, w)])
            A.op("DVE", lambda v, coff=coff, w=w: v.tensor_mul(
                rr(VAR, coff, w).ap, rr(VAR, coff, w).ap, rr(RSTD, coff, w).ap),
                reads=[rr(VAR, coff, w), rr(RSTD, coff, w)],
                writes=[rr(VAR, coff, w)])
            A.op("DVE", lambda v, coff=coff, w=w: v.tensor_scalar_add(
                rr(VAR, coff, w).ap, rr(VAR, coff, w).ap, 1e-5),
                reads=[rr(VAR, coff, w)], writes=[rr(VAR, coff, w)])
            A.op("DVE", lambda v, coff=coff, w=w: v.tensor_mul(
                rr(VAR, coff, w).ap, rr(VAR, coff, w).ap, rr(MR, coff, w).ap),
                reads=[rr(VAR, coff, w), rr(MR, coff, w)], writes=[rr(VAR, coff, w)])
            A.op("ACT", lambda sc, coff=coff, w=w: sc.activation(
                rr(RSTD, coff, w).ap, rr(VAR, coff, w).ap, AF.Ln),
                reads=[rr(VAR, coff, w)], writes=[rr(RSTD, coff, w)])
            A.op("ACT", lambda sc, coff=coff, w=w: sc.activation(
                rr(RSTD, coff, w).ap, rr(RSTD, coff, w).ap, AF.Exp, scale=-0.5),
                reads=[rr(RSTD, coff, w)], writes=[rr(RSTD, coff, w)])
            A.op("DVE", lambda v, coff=coff, w=w: v.tensor_mul(
                rr(MR, coff, w).ap, rr(MEAN, coff, w).ap, rr(RSTD, coff, w).ap),
                reads=[rr(MEAN, coff, w), rr(RSTD, coff, w)],
                writes=[rr(MR, coff, w)])
        A.op("PE", lambda te: te.matmul(PS[2][:], onesrs[:], rr(RSTD).ap,
             start=True, stop=True), reads=[rr(RSTD), R(onesrs)], writes=[R(PS[2])])
        A.op("PE", lambda te: te.matmul(PS[4][:, 0:3], onesrs[:],
             rr(RSTD, 512, 3).ap, start=True, stop=True),
             reads=[rr(RSTD, 512, 3), R(onesrs)], writes=[R(PS[4], 0, 128, 0, 3)])
        A.op("PE", lambda te: te.matmul(PS[3][:], onesrs[:], rr(MR).ap,
             start=True, stop=True), reads=[rr(MR), R(onesrs)], writes=[R(PS[3])])
        A.op("PE", lambda te: te.matmul(PS[4][:, 4:7], onesrs[:], rr(MR, 512, 3).ap,
             start=True, stop=True), reads=[rr(MR, 512, 3), R(onesrs)],
             writes=[R(PS[4], 0, 128, 4, 3)])
        for dg in range(NDG):
            xr = R(x2b, 0, 128, dg * 515, 515)
            hr = R(hb3, 0, 128, dg * 515, 515)
            sq = R(sqb, 0, 128, (dg % 2) * 515, 515)
            rcol = R(rms1s, 0, 128, dg, 1)
            A.op("DVE", lambda v, xr=xr, sq=sq: v.tensor_mul(
                sq.t[0:128, sq.c0:sq.c0 + 512], xr.t[0:128, xr.c0:xr.c0 + 512],
                PS[2][:]), reads=[xr, R(PS[2])], writes=[sq])
            A.op("DVE", lambda v, xr=xr, sq=sq: v.tensor_mul(
                sq.t[0:128, sq.c0 + 512:sq.c0 + 515],
                xr.t[0:128, xr.c0 + 512:xr.c0 + 515], PS[4][:, 0:3]),
                reads=[xr, R(PS[4], 0, 128, 0, 3)], writes=[sq])
            A.op("DVE", lambda v, sq=sq: v.tensor_sub(
                sq.t[0:128, sq.c0:sq.c0 + 512], sq.t[0:128, sq.c0:sq.c0 + 512],
                PS[3][:]), reads=[sq, R(PS[3])], writes=[sq])
            A.op("DVE", lambda v, sq=sq: v.tensor_sub(
                sq.t[0:128, sq.c0 + 512:sq.c0 + 515],
                sq.t[0:128, sq.c0 + 512:sq.c0 + 515], PS[4][:, 4:7]),
                reads=[sq, R(PS[4], 0, 128, 4, 3)], writes=[sq])
            A.op("DVE", lambda v, sq=sq, hr=hr, rcol=rcol: v.tensor_scalar(
                hr.ap, sq.t[0:128, sq.c0:sq.c0 + 515], rcol.ap, None, ALU.mult),
                reads=[sq], writes=[hr], sreads=[rcol])

        if j == 0:
            for (wsrc, nfg, goff) in ((wq, NQG, 0), (wk, NKG, NQG),
                                      (wv, NVG, NQG + NKG)):
                for fg in range(nfg):
                    pr = R(PS[5], 0, 128, 0, 3)
                    proj_psum(pr, wsrc, fg, fg % 2, 0, 3)
                    zt = R(ztail, 0, 128, (goff + fg) * 3 % (64 * 3), 3) \
                        if False else R(ztail, 0, 128, ((goff + fg) % 64) * 3, 3)
                    A.op("DVE", lambda v, zt=zt, pr=pr: v.tensor_copy(zt.ap, pr.ap),
                         reads=[pr], writes=[zt])

        def conv_silu(fg, gslot, cws_t):
            zt = R(ztail, 0, 128, gslot * 3, 3)
            w4 = [R(cws_t, 0, 128, fg * 4 + k, 1) for k in range(4)]
            A.op("DVE", lambda v, w=w4[3]: v.tensor_scalar(cvb[:], zb[:], w.ap,
                 None, ALU.mult), reads=[R(zb)], writes=[R(cvb)], sreads=[w4[3]])
            for k in range(3):
                nz = 3 - k
                A.op("DVE", lambda v, w=w4[k], nz=nz, zt=zt, k=k:
                     v.scalar_tensor_tensor(cvb[:, 0:nz],
                     zt.t[0:128, zt.c0 + k:zt.c0 + 3], w.ap, cvb[:, 0:nz],
                     ALU.mult, ALU.add), reads=[zt, R(cvb, 0, 128, 0, nz)],
                     writes=[R(cvb, 0, 128, 0, nz)], sreads=[w4[k]])
                A.op("DVE", lambda v, w=w4[k], nz=nz: v.scalar_tensor_tensor(
                     cvb[:, nz:TT], zb[:, 0:TT - nz], w.ap, cvb[:, nz:TT],
                     ALU.mult, ALU.add),
                     reads=[R(zb), R(cvb, 0, 128, nz, TT - nz)],
                     writes=[R(cvb, 0, 128, nz, TT - nz)], sreads=[w4[k]])
            A.op("POOL", lambda g, zt=zt: g.tensor_copy(zt.ap, zb[:, TT - 3:TT]),
                 reads=[R(zb, 0, 128, TT - 3, 3)], writes=[zt])
            A.op("ACT", lambda sc: sc.activation(cvb[:], cvb[:], AF.Silu),
                 reads=[R(cvb)], writes=[R(cvb)])

        def l2norm(scale):
            A.op("ACT", lambda sc: sc.activation(sqz[:], cvb[:], AF.Square),
                 reads=[R(cvb)], writes=[R(sqz)])
            A.op("PE", lambda te: te.matmul(PS[1][0:2, :], hdsels[:], sqz[:],
                 start=True, stop=True), reads=[R(sqz), R(hdsels)],
                 writes=[R(PS[1], 0, 2, 0, TT)])
            A.op("DVE", lambda v: v.tensor_scalar_add(rsr[:], PS[1][0:2, :], 1e-6),
                 reads=[R(PS[1], 0, 2, 0, TT)], writes=[R(rsr)])
            A.op("ACT", lambda sc: sc.activation(rsr[:], rsr[:], AF.Ln),
                 reads=[R(rsr)], writes=[R(rsr)])
            sc2 = -0.5
            A.op("ACT", lambda sc: sc.activation(rsr[:], rsr[:], AF.Exp, scale=sc2),
                 reads=[R(rsr)], writes=[R(rsr)])
            if scale != 1.0:
                A.op("DVE", lambda v, scale=scale: v.tensor_scalar_mul(rsr[:],
                     rsr[:], scale), reads=[R(rsr)], writes=[R(rsr)])
            A.op("PE", lambda te: te.matmul(PS[1][:], sel2s[:], rsr[:],
                 start=True, stop=True), reads=[R(rsr), R(sel2s)], writes=[R(PS[1])])

        for fg in range(NQG):
            pr = R(PS[0], 0, 128, 0, TT)
            proj_psum(pr, wq, fg, fg % 2, 3, 512)
            A.op("ACT", lambda sc, pr=pr: sc.copy(zb[:], pr.ap), reads=[pr],
                 writes=[R(zb)])
            conv_silu(fg, fg, cwqs)
            l2norm(48.0 ** -0.5)
            A.op("DVE", lambda v: v.tensor_mul(qtb[:], cvb[:], PS[1][:]),
                 reads=[R(cvb), R(PS[1])], writes=[R(qtb)])
            A.dma("SP", qn_o[fg * 128:(fg + 1) * 128, xlo:xlo + TT], R(qtb))
        for fg in range(NKG):
            pr = R(PS[0], 0, 128, 0, TT)
            proj_psum(pr, wk, fg, fg % 2, 3, 512)
            A.op("ACT", lambda sc, pr=pr: sc.copy(zb[:], pr.ap), reads=[pr],
                 writes=[R(zb)])
            conv_silu(fg, 16 + fg, cwks)
            l2norm(1.0)
            kd = R(knb, 0, 128, fg * TT, TT)
            A.op("DVE", lambda v, kd=kd: v.tensor_mul(kd.ap, cvb[:], PS[1][:]),
                 reads=[R(cvb), R(PS[1])], writes=[kd])
            A.dma("SP", kn_o[fg * 128:(fg + 1) * 128, xlo:xlo + TT], kd)
        for fg in range(NVG):
            pr = R(PS[0], 0, 128, 0, TT)
            proj_psum(pr, wv, fg, fg % 2, 3, 512)
            A.op("ACT", lambda sc, pr=pr: sc.copy(zb[:], pr.ap), reads=[pr],
                 writes=[R(zb)])
            conv_silu(fg, 32 + fg, cwvs)
            vd_ = R(vb3, 0, 128, fg * TT, TT)
            A.op("DVE", lambda v, vd_=vd_: v.tensor_copy(vd_.ap, cvb[:]),
                 reads=[R(cvb)], writes=[vd_])
            A.dma("SP", vo_o[fg * 128:(fg + 1) * 128, xlo:xlo + TT], vd_)
        for fg in range(NVG):
            pr = R(PS[0], 0, 128, 0, TT)
            proj_psum(pr, wg, fg, fg % 2, 3, 512)
            A.op("DVE", lambda v, pr=pr: v.tensor_copy(qtb[:], pr.ap),
                 reads=[pr], writes=[R(qtb)])
            A.dma("SP", go_o[fg * 128:(fg + 1) * 128, xlo:xlo + TT], R(qtb))

        for widx2, wsrc2 in ((0, wa), (1, wb)):
            wb_ = R(wbuf, 0, 128, 0, NDG * NH)
            A.dma("POOL", wb_, wsrc2)
            for kc in range(NDG):
                hr = R(hb3, 0, 128, kc * 515 + 3, 512)
                A.op("PE", lambda te, wb_=wb_, kc=kc, hr=hr, st0=kc == 0,
                     sp=kc == NDG - 1: te.matmul(PS[6][0:NH, :],
                     wb_.t[0:128, wb_.c0 + kc * NH:wb_.c0 + (kc + 1) * NH],
                     hr.ap, start=st0, stop=sp), reads=[wb_, hr],
                     writes=[R(PS[6], 0, NH, 0, TT)])
            if widx2 == 0:
                A.op("DVE", lambda v: v.tensor_scalar(zab[:], PS[6][0:NH, :],
                     dtb32s[:, 0:1], None, ALU.add), reads=[R(PS[6], 0, NH, 0, TT)],
                     writes=[R(zab)], sreads=[R(dtb32s)])
                A.op("DVE", lambda v: v.tensor_scalar_mul(sp1[:], zab[:], -1.0),
                     reads=[R(zab)], writes=[R(sp1)])
                A.op("DVE", lambda v: v.tensor_max(sp2[:], sp1[:], zab[:]),
                     reads=[R(sp1), R(zab)], writes=[R(sp2)])
                A.op("ACT", lambda sc: sc.activation(sp1[:], sp2[:], AF.Exp,
                     scale=-1.0), reads=[R(sp2)], writes=[R(sp1)])
                A.op("ACT", lambda sc: sc.activation(sp1[:], sp1[:], AF.Ln,
                     bias=1.0), reads=[R(sp1)], writes=[R(sp1)])
                A.op("DVE", lambda v: v.tensor_scalar_max(sp2[:], zab[:], 0.0),
                     reads=[R(zab)], writes=[R(sp2)])
                A.op("DVE", lambda v: v.tensor_add(zab[:], sp1[:], sp2[:]),
                     reads=[R(sp1), R(sp2)], writes=[R(zab)])
                A.op("DVE", lambda v: v.tensor_scalar(zab[:], zab[:], eAn[:, 0:1],
                     None, ALU.mult), reads=[R(zab)], writes=[R(zab)],
                     sreads=[R(eAn)])
                gcr = R(gcb, 0, NH, xlo, TT)
                A.op("DVE", lambda v, gcr=gcr: v.tensor_tensor_scan(
                    gcr.ap, zab[:], zab[:], gcst[:, 0:1], ALU.add, ALU.bypass),
                    reads=[R(zab)], writes=[gcr], sreads=[R(gcst)])
                A.op("POOL", lambda g, gcr=gcr: g.tensor_copy(gcst[:],
                     gcr.t[0:NH, gcr.c0 + TT - 1:gcr.c0 + TT]),
                     reads=[gcr], writes=[R(gcst)])
                A.dma("SP", gcum_o[:, xlo:xlo + TT], gcr)
            else:
                ber = R(bet, 0, NH, xlo, TT)
                A.op("ACT", lambda sc, ber=ber: sc.activation(ber.ap,
                     PS[6][0:NH, :], AF.Sigmoid),
                     reads=[R(PS[6], 0, NH, 0, TT)], writes=[ber])
                A.dma("SP", beta_o[:, xlo:xlo + TT], ber)

        for cc in range(TT // CH):
            c0 = xlo + cc * CH
            ccl = cc * CH
            emit_chunk_prep(A, B, PS, idfs, onesrs, el128s, gcb, bet, c0,
                            j == 0 and cc == 0)
            for h in range(NH):
                kg, kp0 = h // 2, (h % 2) * 64
                ktr = R(knb, kp0, 64, kg * TT + ccl, CH)
                vtr = R(vb3, 0, 128, h * TT + ccl, CH)
                sth = R(stv3, kp0, 64, (h // 2) * DVP, DVP)
                emit_chunk_head(A, B, PS, PSB, idbs, ltriss, sth, ktr, vtr, h)

    A.op("DVE", lambda v: v.tensor_copy(sp1[:, 0:1], gcb[0:NH, TC - 1:TC]),
         reads=[R(gcb, 0, NH, TC - 1, 1)], writes=[R(sp1, 0, NH, 0, 1)])
    A.dma("SP", glast_o, R(sp1, 0, NH, 0, 1))
    A.dma("SP", sl_o, R(stv3))
    A.fence()
    return A


def _pad_heads_cols(w, hd, hdp):
    Dd, tot = w.shape
    H = tot // hd
    out = np.zeros((Dd, H * hdp), np.float32)
    for h in range(H):
        out[:, h * hdp:h * hdp + hd] = w[:, h * hd:(h + 1) * hd]
    return out


def _pad_heads_rows(w, hd, hdp):
    tot = w.shape[0]
    H = tot // hd
    out = np.zeros((H * hdp,) + w.shape[1:], np.float32)
    for h in range(H):
        out[h * hdp:h * hdp + hd] = w[h * hd:(h + 1) * hd]
    return out


def _to_pdg(vec):
    return np.ascontiguousarray(np.asarray(vec, np.float32).reshape(NDG, 128).T)


def _cw_layout(cw, ngf):
    return np.ascontiguousarray(cw.reshape(ngf, 128, 4).transpose(1, 0, 2)
                                .reshape(128, ngf * 4))


def stage_launch3(inputs, x2_list):
    wq_h = _cc_layout(_pad_heads_cols(inputs["L1_attn_Wq"], 48, 64))
    wk_h = _cc_layout(_pad_heads_cols(inputs["L1_attn_Wk"], 48, 64))
    wv_h = _cc_layout(_pad_heads_cols(inputs["L1_attn_Wv"], 96, 128))
    wg_h = _cc_layout(_pad_heads_cols(inputs["L1_attn_Wg"], 96, 128))
    wa_h = _cc_layout(inputs["L1_attn_Wa"].astype(np.float32), NH)
    wb_h = _cc_layout(inputs["L1_attn_Wb"].astype(np.float32), NH)
    cwq_h = _cw_layout(_pad_heads_rows(inputs["L1_attn_conv_q_w"], 48, 64), 16)
    cwk_h = _cw_layout(_pad_heads_rows(inputs["L1_attn_conv_k_w"], 48, 64), 16)
    cwv_h = _cw_layout(_pad_heads_rows(inputs["L1_attn_conv_v_w"], 96, 128), 32)
    rms1_h = _to_pdg(inputs["L1_attn_rms1_w"])
    dtb32_h = np.ascontiguousarray(inputs["L1_attn_dt_bias"].reshape(NH, 1))
    alog32_h = np.ascontiguousarray(inputs["L1_attn_A_log"].reshape(NH, 1))
    onesc = np.ones((128, 1), np.float32)
    onesr = np.ones((1, 128), np.float32)
    hdsel = np.zeros((128, 2), np.float32)
    hdsel[0:64, 0] = 1
    hdsel[64:128, 1] = 1
    sel2 = np.zeros((2, 128), np.float32)
    sel2[0, 0:64] = 1
    sel2[1, 64:128] = 1
    sel32 = np.zeros((NH, NH * 128), np.float32)
    for h in range(NH):
        sel32[h, h * 128:(h + 1) * 128] = 1
    ltris = np.tril(np.ones((128, 128), np.float32), -1)
    idb = np.eye(128).astype(BF)
    idf = np.eye(128, dtype=np.float32)
    el128 = np.zeros((128, 1), np.float32)
    el128[127, 0] = 1

    maps = []
    for c in range(NCORES):
        x2T = np.zeros((D, TC + 3), np.float32)
        x2T[:, 3:] = x2_list[c]
        if c > 0:
            x2T[:, 0:3] = x2_list[c - 1][:, TC - 3:TC]
        maps.append(dict(
            x2T=np.ascontiguousarray(x2T), rms1=rms1_h, wq=wq_h, wk=wk_h,
            wv=wv_h, wg=wg_h, wa=wa_h, wb=wb_h, cwq=cwq_h, cwk=cwk_h,
            cwv=cwv_h, dtb32=dtb32_h, alog32=alog32_h, onesc=onesc,
            onesr=onesr, hdsel=hdsel, sel2=sel2, sel32=sel32, ltris=ltris,
            idb=idb, idf=idf, el128=el128))
    return maps


def build_launch4():
    A = Auto("l4")
    TB = 256
    NT = TC // TB
    NCC = TB // CH
    NFC = 44

    x2 = A.inp("x2", [D, TC])
    qn = A.inp("qn", [128, 16 * TC], BF16)
    kn = A.inp("kn", [128, 16 * TC], BF16)
    vo = A.inp("vo", [128, 32 * TC], BF16)
    go = A.inp("go", [128, 32 * TC], BF16)
    gcum = A.inp("gcum", [NH, TC])
    beta = A.inp("beta", [NH, TC])
    ghops = A.inp("ghops", [1, 7 * NH])
    lhops = A.inp("lhops", [128, 7 * 2048])
    rms2 = A.inp("rms2", [128, NDG])
    gnw = A.inp("gnw", [128, 1])
    wo = A.inp("wo", [128, 16 * 4096])
    w1i = A.inp("w1i", [128, 44 * 2048])
    w3i = A.inp("w3i", [128, 44 * 2048])
    w2i = A.inp("w2i", [128, 4 * 16 * 1408])
    gw = A.inp("gw", [128, NDG * 8])
    fw1 = A.inp("fw1", [128, 44 * 2048])
    fw3 = A.inp("fw3", [128, 44 * 2048])
    fw2 = A.inp("fw2", [128, 4 * 16 * 1408])
    a1i = A.inp("a1i", [128, 16 * 128])
    a3i = A.inp("a3i", [128, 16 * 128])
    b1c = A.inp("b1c", [16, NFC * 8 * 128])
    b3c = A.inp("b3c", [16, NFC * 8 * 128])
    a2i = A.inp("a2i", [128, NFC * 128])
    b2h = A.inp("b2h", [16, NDG * 8 * 128])
    onesc = A.inp("onesc", [128, 1])
    onesr = A.inp("onesr", [1, 128])
    sel32 = A.inp("sel32", [NH, NH * 128])
    sel8 = A.inp("sel8", [8, 8 * 128], BF16)
    ltris = A.inp("ltris", [128, 128])
    utrii = A.inp("utrii", [128, 128])
    idb = A.inp("idb", [128, 128], BF16)
    idf = A.inp("idf", [128, 128])
    el128 = A.inp("el128", [128, 1])

    x4_o = A.out("x4", [D, TC])

    h2d = A.dram("h2d", [D, TC])
    x3d = A.dram("x3d", [D, TC])

    rms2s = A.sbuf("rms2s", [128, NDG], F32)
    gnws = A.sbuf("gnws", [128, 1], F32)
    onescs = A.sbuf("onescs", [128, 1], F32)
    onesrs = A.sbuf("onesrs", [1, 128], F32)
    sel32s = A.sbuf("sel32s", [NH, NH * 128], F32)
    A._sel32s = sel32s
    sel8s = A.sbuf("sel8s", [8, 8 * 128], BF16)
    ltriss = A.sbuf("ltriss", [128, 128], F32)
    utriis = A.sbuf("utriis", [128, 128], F32)
    idbs = A.sbuf("idbs", [128, 128], BF16)
    idfs = A.sbuf("idfs", [128, 128], F32)
    el128s = A.sbuf("el128s", [128, 1], F32)
    ghs = A.sbuf("ghs", [1, 7 * NH], F32)
    edr = A.sbuf("edr", [128, NH], F32)
    gtl = A.sbuf("gtl", [NH, 1 + TB], F32)
    btl = A.sbuf("btl", [NH, 1 + TB], F32)
    stv4 = A.sbuf("stv4", [128, 16 * DVP], F32)
    lbuf = A.sbuf("lbuf", [128, 1024], F32)
    B = GdrBufs(A)
    qcs = A.sbuf("qcs", [128, 16 * CH], BF16)
    kcs = A.sbuf("kcs", [128, 16 * CH], BF16)
    vcs = A.sbuf("vcs", [128, 32 * CH], BF16)
    gcs = A.sbuf("gcs", [128, 32 * CH], BF16)
    ofw = A.sbuf("ofw", [128, 32 * TB], BF16)
    dmt2 = A.sbuf("dmt2", [128, CH], F32)
    at2 = A.sbuf("at2", [128, CH], BF16)
    lrp = A.sbuf("lrp", [128, CH], F32)
    qhb = A.sbuf("qhb", [128, CH], BF16)
    sqo = A.sbuf("sqo", [128, CH], F32)
    osb = A.sbuf("osb", [128, CH], F32)
    sgo = A.sbuf("sgo", [128, CH], F32)
    ror = A.sbuf("ror", [1, CH], F32)
    wbuf = A.sbuf("wbuf", [128, 2 * 32 * 128], BF16)
    ffa = A.sbuf("ffa", [128, 11 * TB], BF16)
    ffacc = A.sbuf("ffacc", [128, NDG * TB], F32)
    ffb = A.sbuf("ffb", [128, NDG * TB], BF16)
    base1 = A.sbuf("base1", [128, TB], F32)
    base3 = A.sbuf("base3", [128, TB], F32)
    sgl4 = A.sbuf("sgl4", [128, TB], F32)
    h1b = A.sbuf("h1b", [128, TB], F32)
    h3b = A.sbuf("h3b", [128, TB], F32)
    awf = A.sbuf("awf", [128, TB], F32)
    awm = A.sbuf("awm", [128, TB], BF16)
    mixf = A.sbuf("mixf", [128, TB], F32)
    xrl = A.sbuf("xrl", [128, 2 * TB], F32)
    hrl = A.sbuf("hrl", [128, 2 * TB], F32)
    sqb4 = A.sbuf("sqb4", [128, 2 * TB], F32)
    srow = A.sbuf("srow", [1, 4 * TB], F32)
    lgt = A.sbuf("lgt", [128, NCC * 8], F32)
    m1b = A.sbuf("m1b", [128, 8], F32)
    geb = A.sbuf("geb", [128, NCC * 8], F32)
    exb = A.sbuf("exb", [128, NCC * 8], F32)
    wqb = A.sbuf("wqb", [128, NCC * 8], F32)
    wtm4 = A.sbuf("wtm4", [8, TB], BF16)
    wrep = A.sbuf("wrep", [128, 8 * TB], BF16)
    u1bx = A.sbuf("u1bx", [16, 8 * TB], BF16)
    u3bx = A.sbuf("u3bx", [16, 8 * TB], BF16)
    zacc = A.sbuf("zacc", [16, 8 * TB], F32)
    zbe = A.sbuf("zbe", [16, 8 * TB], BF16)
    a1s = A.sbuf("a1s", [128, 16 * 128], BF16)
    a3s = A.sbuf("a3s", [128, 16 * 128], BF16)
    a2s = A.sbuf("a2s", [128, NFC * 128], BF16)
    bb1 = A.sbuf("bb1", [16, 8 * 128], BF16)
    bb3 = A.sbuf("bb3", [16, 8 * 128], BF16)
    b2b = A.sbuf("b2b", [16, 8 * 128], BF16)

    PS = [A.psum(f"PS{i}", [128, 512]) for i in range(7)]
    PSB = A.psum("PSB", [128, 1024], BF16)

    for t, s in ((rms2s, rms2), (gnws, gnw), (onescs, onesc), (onesrs, onesr),
                 (sel32s, sel32), (ltriss, ltris), (utriis, utrii),
                 (idfs, idf), (el128s, el128), (ghs, ghops)):
        A.dma("SP", R(t), s)
    A.dma("SP", R(sel8s), sel8)
    A.dma("SP", R(idbs), idb)
    A.dma("POOL", R(a1s), a1i)
    A.dma("POOL", R(a3s), a3i)
    A.dma("POOL", R(a2s), a2i)
    A.op("DVE", lambda v: v.tensor_copy(B.id64[0:64, 0:64], idbs[0:64, 0:64]),
         reads=[R(idbs, 0, 64, 0, 64)], writes=[R(B.id64, 0, 64, 0, 64)])
    A.op("DVE", lambda v: v.tensor_copy(B.id64[64:128, 0:64], idbs[64:128, 64:128]),
         reads=[R(idbs, 64, 64, 64, 64)], writes=[R(B.id64, 64, 64, 0, 64)])

    A.op("ACT", lambda sc: sc.activation(ghs[:], ghs[:], AF.Exp),
         reads=[R(ghs)], writes=[R(ghs)])
    A.op("POOL", lambda g: g.memset(stv4[:], 0.0), writes=[R(stv4)])
    for hop in range(7):
        A.op("PE", lambda te, hop=hop: te.matmul(PS[0][0:128, 0:NH], onesrs[:],
             ghs[0:1, hop * NH:(hop + 1) * NH], start=True, stop=True),
             reads=[R(ghs), R(onesrs)], writes=[R(PS[0], 0, 128, 0, NH)])
        A.op("DVE", lambda v: v.tensor_copy(edr[:], PS[0][0:128, 0:NH]),
             reads=[R(PS[0], 0, 128, 0, NH)], writes=[R(edr)])
        for half in range(2):
            lb = R(lbuf)
            A.dma("SP", lb, lhops[:, hop * 2048 + half * 1024:
                                  hop * 2048 + (half + 1) * 1024])
            for hh in range(16):
                h = half * 16 + hh
                pb = (h % 2) * 64
                sth = R(stv4, pb, 64, (h // 2) * DVP, DVP)
                lr = R(lbuf, pb, 64, (hh // 2) * DVP, DVP)
                ec = R(edr, pb, 64, h, 1)
                A.op("DVE", lambda v, sth=sth, lr=lr, ec=ec:
                     v.scalar_tensor_tensor(sth.ap, sth.ap, ec.ap, lr.ap,
                     ALU.mult, ALU.add), reads=[sth, lr], writes=[sth],
                     sreads=[ec])

    MEAN, VAR, RSTD, MR = 0, TB, 2 * TB, 3 * TB

    def rr(off, n=TB):
        return R(srow, 0, 1, off, n)

    def emit_ln_stats(src_dram, xlo):
        for dg in range(NDG):
            xr = R(xrl, 0, 128, (dg % 2) * TB, TB)
            A.dma("SP", xr, src_dram[dg * 128:(dg + 1) * 128, xlo:xlo + TB])
            sq = R(sqb4, 0, 128, (dg % 2) * TB, TB)
            A.op("ACT", lambda sc, xr=xr, sq=sq: sc.activation(sq.ap, xr.ap,
                 AF.Square), reads=[xr], writes=[sq])
            st0, sp = dg == 0, dg == NDG - 1
            A.op("PE", lambda te, xr=xr, st0=st0, sp=sp: te.matmul(
                PS[2][0:1, 0:TB], onescs[:], xr.ap, start=st0, stop=sp),
                reads=[xr, R(onescs)], writes=[R(PS[2], 0, 1, 0, TB)])
            A.op("PE", lambda te, sq=sq, st0=st0, sp=sp: te.matmul(
                PS[3][0:1, 0:TB], onescs[:], sq.ap, start=st0, stop=sp),
                reads=[sq, R(onescs)], writes=[R(PS[3], 0, 1, 0, TB)])
        inv_d = 1.0 / D
        A.op("DVE", lambda v: v.tensor_scalar_mul(rr(MEAN).ap, PS[2][0:1, 0:TB],
             inv_d), reads=[R(PS[2], 0, 1, 0, TB)], writes=[rr(MEAN)])
        A.op("DVE", lambda v: v.tensor_mul(rr(VAR).ap, rr(MEAN).ap,
             rr(MEAN).ap), reads=[rr(MEAN)], writes=[rr(VAR)])
        A.op("DVE", lambda v: v.scalar_tensor_tensor(rr(VAR).ap, PS[3][0:1, 0:TB],
             inv_d, rr(VAR).ap, ALU.mult, ALU.subtract),
             reads=[R(PS[3], 0, 1, 0, TB), rr(VAR)], writes=[rr(VAR)])
        A.op("DVE", lambda v: v.tensor_scalar_add(rr(VAR).ap, rr(VAR).ap, 1e-5),
             reads=[rr(VAR)], writes=[rr(VAR)])
        A.op("ACT", lambda sc: sc.activation(rr(RSTD).ap, rr(VAR).ap, AF.Ln),
             reads=[rr(VAR)], writes=[rr(RSTD)])
        A.op("ACT", lambda sc: sc.activation(rr(RSTD).ap, rr(RSTD).ap, AF.Exp,
             scale=-0.5), reads=[rr(RSTD)], writes=[rr(RSTD)])
        A.op("DVE", lambda v: v.tensor_mul(rr(MR).ap, rr(MEAN).ap,
             rr(RSTD).ap), reads=[rr(MEAN), rr(RSTD)], writes=[rr(MR)])
        A.op("PE", lambda te: te.matmul(PS[2][0:128, 0:TB], onesrs[:],
             rr(RSTD).ap, start=True, stop=True),
             reads=[rr(RSTD), R(onesrs)], writes=[R(PS[2], 0, 128, 0, TB)])
        A.op("PE", lambda te: te.matmul(PS[3][0:128, 0:TB], onesrs[:],
             rr(MR).ap, start=True, stop=True), reads=[rr(MR), R(onesrs)],
             writes=[R(PS[3], 0, 128, 0, TB)])

    for j in range(NT):
        xlo = j * TB
        if j == 0:
            A.dma("SP", R(gtl, 0, NH, 1, TB), gcum[:, 0:TB])
            A.dma("SP", R(btl, 0, NH, 1, TB), beta[:, 0:TB])
        else:
            A.dma("SP", R(gtl), gcum[:, xlo - 1:xlo + TB])
            A.dma("SP", R(btl), beta[:, xlo - 1:xlo + TB])
        for cc in range(NCC):
            c0g = xlo + cc * CH
            ci = c0g // CH
            for (buf, src, g_) in ((qcs, qn, 16), (kcs, kn, 16), (vcs, vo, 32),
                                   (gcs, go, 32)):
                A.dma("SP", R(buf), src[:, ci * g_ * 128:(ci + 1) * g_ * 128])
            emit_chunk_prep(A, B, PS, idfs, onesrs, el128s, gtl, btl,
                            1 + cc * CH, j == 0 and cc == 0)

            def emit_output(h, pb, bTc, lTc, cc=cc):
                qtr = R(qcs, pb, 64, (h // 2) * CH, CH)
                ktr2 = R(kcs, pb, 64, (h // 2) * CH, CH)
                gtr = R(gcs, 0, 128, h * CH, CH)
                ofr = R(ofw, 0, 128, h * TB + cc * CH, CH)
                sbfr = R(B.sbf, pb, 64, 0, DVP)
                A.op("DVE", lambda v, bTc=bTc: v.tensor_scalar(dmt2[:],
                     PS[1][0:128, 0:CH], bTc.ap, None, ALU.subtract),
                     reads=[R(PS[1], 0, 128, 0, CH)], writes=[R(dmt2)],
                     sreads=[bTc])
                A.op("DVE", lambda v: v.tensor_scalar_min(dmt2[:], dmt2[:], 0.0),
                     reads=[R(dmt2)], writes=[R(dmt2)])
                A.op("ACT", lambda sc: sc.activation(dmt2[:], dmt2[:], AF.Exp),
                     reads=[R(dmt2)], writes=[R(dmt2)])
                A.op("DVE", lambda v: v.tensor_mul(dmt2[:], dmt2[:], utriis[:]),
                     reads=[R(dmt2), R(utriis)], writes=[R(dmt2)])
                A.op("PE", lambda te, ktr2=ktr2, qtr=qtr: te.matmul(
                    PS[4][0:128, 0:CH], ktr2.ap, qtr.ap, start=True, stop=True),
                    reads=[ktr2, qtr], writes=[R(PS[4], 0, 128, 0, CH)])
                A.op("DVE", lambda v: v.tensor_mul(at2[:], dmt2[:],
                     PS[4][0:128, 0:CH]), reads=[R(dmt2), R(PS[4], 0, 128, 0, CH)],
                     writes=[R(at2)])
                A.op("ACT", lambda sc: sc.activation(lrp[:], PS[1][0:128, 0:CH],
                     AF.Exp), reads=[R(PS[1], 0, 128, 0, CH)], writes=[R(lrp)])
                A.op("DVE", lambda v, qtr=qtr, pb=pb: v.tensor_mul(
                    qhb[pb:pb + 64, :], qtr.ap, lrp[pb:pb + 64, :]),
                    reads=[qtr, R(lrp, pb, 64, 0, CH)],
                    writes=[R(qhb, pb, 64, 0, CH)])
                A.op("PE", lambda te: te.matmul(PS[6][0:128, 0:CH], B.ub[:],
                     at2[:], start=True, stop=False), reads=[R(B.ub), R(at2)],
                     writes=[R(PS[6], 0, 128, 0, CH)])
                A.op("PE", lambda te, sbfr=sbfr, pb=pb: te.matmul(
                    PS[6][0:128, 0:CH], sbfr.ap, qhb[pb:pb + 64, :],
                    start=False, stop=True),
                    reads=[sbfr, R(qhb, pb, 64, 0, CH)],
                    writes=[R(PS[6], 0, 128, 0, CH)])
                A.op("ACT", lambda sc: sc.activation(sqo[:], PS[6][0:128, 0:CH],
                     AF.Square), reads=[R(PS[6], 0, 128, 0, CH)], writes=[R(sqo)])
                A.op("PE", lambda te: te.matmul(PS[4][0:1, 0:CH], onescs[:],
                     sqo[:], start=True, stop=True), reads=[R(sqo), R(onescs)],
                     writes=[R(PS[4], 0, 1, 0, CH)])
                A.op("DVE", lambda v: v.tensor_scalar(ror[:], PS[4][0:1, 0:CH],
                     1.0 / 96.0, 1e-5, ALU.mult, ALU.add),
                     reads=[R(PS[4], 0, 1, 0, CH)], writes=[R(ror)])
                A.op("ACT", lambda sc: sc.activation(ror[:], ror[:], AF.Ln),
                     reads=[R(ror)], writes=[R(ror)])
                A.op("ACT", lambda sc: sc.activation(ror[:], ror[:], AF.Exp,
                     scale=-0.5), reads=[R(ror)], writes=[R(ror)])
                A.op("PE", lambda te: te.matmul(PS[4][0:128, 0:CH], onesrs[:],
                     ror[:], start=True, stop=True), reads=[R(ror), R(onesrs)],
                     writes=[R(PS[4], 0, 128, 0, CH)])
                A.op("ACT", lambda sc: sc.copy(osb[:], PS[6][0:128, 0:CH]),
                     reads=[R(PS[6], 0, 128, 0, CH)], writes=[R(osb)])
                A.op("DVE", lambda v: v.tensor_mul(osb[:], osb[:],
                     PS[4][0:128, 0:CH]), reads=[R(osb), R(PS[4], 0, 128, 0, CH)],
                     writes=[R(osb)])
                A.op("ACT", lambda sc, gtr=gtr: sc.activation(sgo[:], gtr.ap,
                     AF.Silu), reads=[gtr], writes=[R(sgo)])
                A.op("DVE", lambda v, ofr=ofr: v.scalar_tensor_tensor(ofr.ap,
                     osb[:], gnws[:, 0:1], sgo[:], ALU.mult, ALU.mult),
                     reads=[R(osb), R(sgo)], writes=[ofr], sreads=[R(gnws)])
            extra = {"emit_output": emit_output}
            for h in range(NH):
                kg, kp0 = h // 2, (h % 2) * 64
                ktr = R(kcs, kp0, 64, kg * CH, CH)
                vtr = R(vcs, 0, 128, h * CH, CH)
                sth = R(stv4, kp0, 64, kg * DVP, DVP)
                emit_chunk_head(A, B, PS, PSB, idbs, ltriss, sth, ktr, vtr, h,
                                extra=extra)

        emit_ln_stats(x2, xlo)
        for dg in range(NDG):
            wb_ = R(wbuf, 0, 128, (dg % 2) * 4096, 4096)
            A.dma("POOL", wb_, wo[:, dg * 4096:(dg + 1) * 4096])
            for kc in range(32):
                lh = R(wbuf, 0, 128, (dg % 2) * 4096 + kc * 128, 128)
                ofr = R(ofw, 0, 128, kc * TB, TB)
                A.op("PE", lambda te, lh=lh, ofr=ofr, st0=kc == 0, sp=kc == 31:
                     te.matmul(PS[0][0:128, 0:TB], lh.ap, ofr.ap, start=st0,
                     stop=sp), reads=[lh, ofr], writes=[R(PS[0], 0, 128, 0, TB)])
            xr = R(xrl, 0, 128, (dg % 2) * TB, TB)
            hr = R(hrl, 0, 128, (dg % 2) * TB, TB)
            sq = R(sqb4, 0, 128, (dg % 2) * TB, TB)
            A.dma("SP", xr, x2[dg * 128:(dg + 1) * 128, xlo:xlo + TB])
            A.op("DVE", lambda v, xr=xr, sq=sq: v.tensor_mul(sq.ap, xr.ap,
                 PS[2][0:128, 0:TB]), reads=[xr, R(PS[2], 0, 128, 0, TB)],
                 writes=[sq])
            A.op("DVE", lambda v, sq=sq, hr=hr: v.tensor_sub(hr.ap, sq.ap,
                 PS[3][0:128, 0:TB]), reads=[sq, R(PS[3], 0, 128, 0, TB)],
                 writes=[hr])
            A.op("DVE", lambda v, hr=hr: v.tensor_add(hr.ap, hr.ap,
                 PS[0][0:128, 0:TB]), reads=[hr, R(PS[0], 0, 128, 0, TB)],
                 writes=[hr])
            A.dma("SP", h2d[dg * 128:(dg + 1) * 128, xlo:xlo + TB], hr)
            A.op("ACT", lambda sc, hr=hr, sq=sq: sc.activation(sq.ap, hr.ap,
                 AF.Square), reads=[hr], writes=[sq])
            A.op("PE", lambda te, sq=sq, st0=dg == 0, sp=dg == NDG - 1:
                 te.matmul(PS[4][0:1, 0:TB], onescs[:], sq.ap, start=st0,
                 stop=sp), reads=[sq, R(onescs)], writes=[R(PS[4], 0, 1, 0, TB)])
        A.op("DVE", lambda v: v.tensor_scalar(rr(VAR).ap, PS[4][0:1, 0:TB],
             1.0 / D, 1e-5, ALU.mult, ALU.add), reads=[R(PS[4], 0, 1, 0, TB)],
             writes=[rr(VAR)])
        A.op("ACT", lambda sc: sc.activation(rr(RSTD).ap, rr(VAR).ap, AF.Ln),
             reads=[rr(VAR)], writes=[rr(RSTD)])
        A.op("ACT", lambda sc: sc.activation(rr(RSTD).ap, rr(RSTD).ap, AF.Exp,
             scale=-0.5), reads=[rr(RSTD)], writes=[rr(RSTD)])
        A.op("PE", lambda te: te.matmul(PS[4][0:128, 0:TB], onesrs[:],
             rr(RSTD).ap, start=True, stop=True), reads=[rr(RSTD), R(onesrs)],
             writes=[R(PS[4], 0, 128, 0, TB)])
        for dg in range(NDG):
            hr = R(hrl, 0, 128, (dg % 2) * TB, TB)
            sq = R(sqb4, 0, 128, (dg % 2) * TB, TB)
            fr = R(ffb, 0, 128, dg * TB, TB)
            rcol = R(rms2s, 0, 128, dg, 1)
            A.dma("SP", hr, h2d[dg * 128:(dg + 1) * 128, xlo:xlo + TB])
            A.op("DVE", lambda v, hr=hr, sq=sq: v.tensor_mul(sq.ap, hr.ap,
                 PS[4][0:128, 0:TB]), reads=[hr, R(PS[4], 0, 128, 0, TB)],
                 writes=[sq])
            A.op("DVE", lambda v, sq=sq, fr=fr, rcol=rcol: v.tensor_scalar(
                fr.ap, sq.ap, rcol.ap, None, ALU.mult), reads=[sq], writes=[fr],
                sreads=[rcol])
        for grp in range(4):
            for icl in range(11):
                fc = grp * 11 + icl
                w1b = R(wbuf, 0, 128, 0, 2048)
                w3b = R(wbuf, 0, 128, 2048, 2048)
                A.dma("POOL", w1b, w1i[:, fc * 2048:(fc + 1) * 2048])
                A.dma("POOL", w3b, w3i[:, fc * 2048:(fc + 1) * 2048])
                for kc in range(NDG):
                    l1 = R(wbuf, 0, 128, kc * 128, 128)
                    l3 = R(wbuf, 0, 128, 2048 + kc * 128, 128)
                    fbr = R(ffb, 0, 128, kc * TB, TB)
                    st0, sp = kc == 0, kc == NDG - 1
                    A.op("PE", lambda te, l1=l1, fbr=fbr, st0=st0, sp=sp:
                         te.matmul(PS[0][0:128, 0:TB], l1.ap, fbr.ap,
                         start=st0, stop=sp), reads=[l1, fbr],
                         writes=[R(PS[0], 0, 128, 0, TB)])
                    A.op("PE", lambda te, l3=l3, fbr=fbr, st0=st0, sp=sp:
                         te.matmul(PS[1][0:128, 0:TB], l3.ap, fbr.ap,
                         start=st0, stop=sp), reads=[l3, fbr],
                         writes=[R(PS[1], 0, 128, 0, TB)])
                A.op("ACT", lambda sc: sc.activation(sgl4[:], PS[0][0:128, 0:TB],
                     AF.Silu), reads=[R(PS[0], 0, 128, 0, TB)], writes=[R(sgl4)])
                ff_ = R(ffa, 0, 128, icl * TB, TB)
                A.op("DVE", lambda v, ff_=ff_: v.tensor_mul(ff_.ap, sgl4[:],
                     PS[1][0:128, 0:TB]), reads=[R(sgl4), R(PS[1], 0, 128, 0, TB)],
                     writes=[ff_])
            for dg in range(NDG):
                db = R(wbuf, 0, 128, (dg % 2) * 4096, 11 * 128)
                A.dma("POOL", db, w2i[:, (grp * NDG + dg) * 1408:
                                      (grp * NDG + dg + 1) * 1408])
                psd = PS[5 + (dg % 2)]
                for icl in range(11):
                    lw = R(wbuf, 0, 128, (dg % 2) * 4096 + icl * 128, 128)
                    ff_ = R(ffa, 0, 128, icl * TB, TB)
                    A.op("PE", lambda te, lw=lw, ff_=ff_, psd=psd, st0=icl == 0,
                         sp=icl == 10: te.matmul(psd[0:128, 0:TB], lw.ap,
                         ff_.ap, start=st0, stop=sp), reads=[lw, ff_],
                         writes=[R(psd, 0, 128, 0, TB)])
                far = R(ffacc, 0, 128, dg * TB, TB)
                if grp == 0:
                    A.op("DVE", lambda v, far=far, psd=psd: v.tensor_copy(
                        far.ap, psd[0:128, 0:TB]), reads=[R(psd, 0, 128, 0, TB)],
                        writes=[far])
                else:
                    A.op("DVE", lambda v, far=far, psd=psd: v.tensor_add(
                        far.ap, far.ap, psd[0:128, 0:TB]),
                        reads=[far, R(psd, 0, 128, 0, TB)], writes=[far])
        for dg in range(NDG):
            xr = R(xrl, 0, 128, (dg % 2) * TB, TB)
            hr = R(hrl, 0, 128, (dg % 2) * TB, TB)
            far = R(ffacc, 0, 128, dg * TB, TB)
            A.dma("SP", xr, x2[dg * 128:(dg + 1) * 128, xlo:xlo + TB])
            A.dma("SP", hr, h2d[dg * 128:(dg + 1) * 128, xlo:xlo + TB])
            A.op("DVE", lambda v, xr=xr, hr=hr: v.tensor_add(hr.ap, hr.ap,
                 xr.ap), reads=[xr, hr], writes=[hr])
            A.op("DVE", lambda v, hr=hr, far=far: v.tensor_add(hr.ap, hr.ap,
                 far.ap), reads=[hr, far], writes=[hr])
            A.dma("SP", x3d[dg * 128:(dg + 1) * 128, xlo:xlo + TB], hr)

        emit_ln_stats(x3d, xlo)
        for dg in range(NDG):
            xr = R(xrl, 0, 128, (dg % 2) * TB, TB)
            sq = R(sqb4, 0, 128, (dg % 2) * TB, TB)
            lr = R(ffb, 0, 128, dg * TB, TB)
            A.dma("SP", xr, x3d[dg * 128:(dg + 1) * 128, xlo:xlo + TB])
            A.op("DVE", lambda v, xr=xr, sq=sq: v.tensor_mul(sq.ap, xr.ap,
                 PS[2][0:128, 0:TB]), reads=[xr, R(PS[2], 0, 128, 0, TB)],
                 writes=[sq])
            A.op("DVE", lambda v, sq=sq, lr=lr: v.tensor_sub(lr.ap, sq.ap,
                 PS[3][0:128, 0:TB]), reads=[sq, R(PS[3], 0, 128, 0, TB)],
                 writes=[lr])
        wbr = R(wbuf, 0, 128, 0, NDG * 8)
        A.dma("POOL", wbr, gw)
        for tcc in range(NCC):
            lg = R(lgt, 0, 128, tcc * 8, 8)
            for kc in range(NDG):
                xst = R(ffb, 0, 128, kc * TB + tcc * 128, 128)
                wst = R(wbuf, 0, 128, kc * 8, 8)
                A.op("PE", lambda te, xst=xst, wst=wst, st0=kc == 0,
                     sp=kc == NDG - 1: te.matmul(PS[5][0:128, 0:8], xst.ap,
                     wst.ap, start=st0, stop=sp), reads=[xst, wst],
                     writes=[R(PS[5], 0, 128, 0, 8)])
            A.op("DVE", lambda v, lg=lg: v.tensor_copy(lg.ap, PS[5][0:128, 0:8]),
                 reads=[R(PS[5], 0, 128, 0, 8)], writes=[lg])
            m1 = R(m1b, 0, 128, 0, 1)
            l2m = R(m1b, 0, 128, 1, 1)
            m1n = R(m1b, 0, 128, 2, 1)
            rde = R(m1b, 0, 128, 3, 1)
            rdc = R(m1b, 0, 128, 4, 1)
            m1c = R(m1b, 0, 128, 5, 1)
            l2c = R(m1b, 0, 128, 6, 1)
            ge = R(geb, 0, 128, tcc * 8, 8)
            ex = R(exb, 0, 128, tcc * 8, 8)
            wq_ = R(wqb, 0, 128, tcc * 8, 8)
            A.op("DVE", lambda v, lg=lg, m1=m1: v.reduce_max(m1.ap, lg.ap,
                 axis=AX), reads=[lg], writes=[m1])
            A.op("POOL", lambda g, m1=m1, m1n=m1n: g.tensor_scalar_mul(m1n.ap,
                 m1.ap, -1.0), reads=[m1], writes=[m1n])
            A.op("POOL", lambda g, m1=m1, m1c=m1c: g.tensor_copy(m1c.ap, m1.ap),
                 reads=[m1], writes=[m1c])
            A.op("DVE", lambda v, lg=lg, ge=ge, m1c=m1c: v.tensor_scalar(ge.ap,
                 lg.ap, m1c.ap, None, ALU.is_ge), reads=[lg], writes=[ge],
                 sreads=[m1c])
            A.op("DVE", lambda v, ge=ge, lg=lg: v.scalar_tensor_tensor(ge.ap,
                 ge.ap, -1e30, lg.ap, ALU.mult, ALU.add), reads=[ge, lg],
                 writes=[ge])
            A.op("DVE", lambda v, ge=ge, l2m=l2m: v.reduce_max(l2m.ap, ge.ap,
                 axis=AX), reads=[ge], writes=[l2m])
            A.op("POOL", lambda g, l2m=l2m, l2c=l2c: g.tensor_copy(l2c.ap,
                 l2m.ap), reads=[l2m], writes=[l2c])
            A.op("ACT", lambda sc, ex=ex, lg=lg, m1n=m1n: sc.activation(ex.ap,
                 lg.ap, AF.Exp, bias=m1n.ap), reads=[lg], writes=[ex],
                 sreads=[m1n])
            A.op("ACT", lambda sc, rde=rde, l2m=l2m, m1n=m1n: sc.activation(
                rde.ap, l2m.ap, AF.Exp, bias=m1n.ap), reads=[l2m], writes=[rde],
                sreads=[m1n])
            A.op("DVE", lambda v, rde=rde: v.tensor_scalar_add(rde.ap, rde.ap,
                 1.0), reads=[rde], writes=[rde])
            A.op("DVE", lambda v, rde=rde: v.reciprocal(rde.ap, rde.ap),
                 reads=[rde], writes=[rde])
            A.op("POOL", lambda g, rde=rde, rdc=rdc: g.tensor_copy(rdc.ap,
                 rde.ap), reads=[rde], writes=[rdc])
            A.op("DVE", lambda v, lg=lg, ge=ge, l2c=l2c: v.tensor_scalar(ge.ap,
                 lg.ap, l2c.ap, None, ALU.is_ge), reads=[lg], writes=[ge],
                 sreads=[l2c])
            A.op("DVE", lambda v, ge=ge, ex=ex: v.tensor_mul(ex.ap, ex.ap,
                 ge.ap), reads=[ge, ex], writes=[ex])
            A.op("DVE", lambda v, wq_=wq_, ex=ex, rdc=rdc: v.tensor_scalar(
                wq_.ap, ex.ap, rdc.ap, None, ALU.mult), reads=[ex],
                writes=[wq_], sreads=[rdc])
            A.op("PE", lambda te, wq_=wq_: te.transpose(PS[5][0:8, 0:128],
                 wq_.ap, idfs[:]), reads=[wq_, R(idfs)],
                 writes=[R(PS[5], 0, 8, 0, 128)])
            A.op("ACT", lambda sc, tcc=tcc: sc.copy(
                 wtm4[0:8, tcc * 128:(tcc + 1) * 128], PS[5][0:8, 0:128]),
                 reads=[R(PS[5], 0, 8, 0, 128)],
                 writes=[R(wtm4, 0, 8, tcc * 128, 128)])
        for e in range(8):
            sel = R(sel8s, 0, 8, e * 128, 128)
            rg = R(wrep, 0, 128, e * TB, TB)
            A.op("PE", lambda te, sel=sel: te.matmul(PS[5][0:128, 0:TB], sel.ap,
                 wtm4[:], start=True, stop=True), reads=[sel, R(wtm4)],
                 writes=[R(PS[5], 0, 128, 0, TB)])
            A.op("ACT", lambda sc, rg=rg: sc.copy(rg.ap, PS[5][0:128, 0:TB]),
                 reads=[R(PS[5], 0, 128, 0, TB)], writes=[rg])
        for (ubx, asx) in ((u1bx, a1s), (u3bx, a3s)):
            for e in range(8):
                for kc in range(NDG):
                    lw = R(asx, 0, 128, kc * 128 + e * 16, 16)
                    fbr = R(ffb, 0, 128, kc * TB, TB)
                    A.op("PE", lambda te, lw=lw, fbr=fbr, st0=kc == 0,
                         sp=kc == NDG - 1: te.matmul(PS[5][0:16, 0:TB], lw.ap,
                         fbr.ap, start=st0, stop=sp), reads=[lw, fbr],
                         writes=[R(PS[5], 0, 16, 0, TB)])
                ur = R(ubx, 0, 16, e * TB, TB)
                A.op("DVE", lambda v, ur=ur: v.tensor_scalar_mul(ur.ap,
                     PS[5][0:16, 0:TB], 2.0), reads=[R(PS[5], 0, 16, 0, TB)],
                     writes=[ur])
        A.op("POOL", lambda g: g.memset(zacc[:], 0.0), writes=[R(zacc)])
        for grp in range(4):
            for icl in range(11):
                fc = grp * 11 + icl
                w1b = R(wbuf, 0, 128, 0, 2048)
                w3b = R(wbuf, 0, 128, 2048, 2048)
                A.dma("POOL", w1b, fw1[:, fc * 2048:(fc + 1) * 2048])
                A.dma("POOL", w3b, fw3[:, fc * 2048:(fc + 1) * 2048])
                A.dma("POOL", R(bb1), b1c[:, fc * 1024:(fc + 1) * 1024])
                A.dma("POOL", R(bb3), b3c[:, fc * 1024:(fc + 1) * 1024])
                for kc in range(NDG):
                    l1 = R(wbuf, 0, 128, kc * 128, 128)
                    l3 = R(wbuf, 0, 128, 2048 + kc * 128, 128)
                    fbr = R(ffb, 0, 128, kc * TB, TB)
                    st0, sp = kc == 0, kc == NDG - 1
                    A.op("PE", lambda te, l1=l1, fbr=fbr, st0=st0, sp=sp:
                         te.matmul(PS[0][0:128, 0:TB], l1.ap, fbr.ap,
                         start=st0, stop=sp), reads=[l1, fbr],
                         writes=[R(PS[0], 0, 128, 0, TB)])
                    A.op("PE", lambda te, l3=l3, fbr=fbr, st0=st0, sp=sp:
                         te.matmul(PS[1][0:128, 0:TB], l3.ap, fbr.ap,
                         start=st0, stop=sp), reads=[l3, fbr],
                         writes=[R(PS[1], 0, 128, 0, TB)])
                A.op("ACT", lambda sc: sc.copy(base1[:], PS[0][0:128, 0:TB]),
                     reads=[R(PS[0], 0, 128, 0, TB)], writes=[R(base1)])
                A.op("ACT", lambda sc: sc.copy(base3[:], PS[1][0:128, 0:TB]),
                     reads=[R(PS[1], 0, 128, 0, TB)], writes=[R(base3)])
                for e in range(8):
                    lb1 = R(bb1, 0, 16, e * 128, 128)
                    lb3 = R(bb3, 0, 16, e * 128, 128)
                    ur1 = R(u1bx, 0, 16, e * TB, TB)
                    ur3 = R(u3bx, 0, 16, e * TB, TB)
                    wrr = R(wrep, 0, 128, e * TB, TB)
                    zr = R(zacc, 0, 16, e * TB, TB)
                    A.op("PE", lambda te, lb1=lb1, ur1=ur1: te.matmul(
                        PS[0][0:128, 0:TB], lb1.ap, ur1.ap, start=True,
                        stop=True), reads=[lb1, ur1],
                        writes=[R(PS[0], 0, 128, 0, TB)])
                    A.op("DVE", lambda v: v.tensor_add(h1b[:], base1[:],
                         PS[0][0:128, 0:TB]), reads=[R(base1),
                         R(PS[0], 0, 128, 0, TB)], writes=[R(h1b)])
                    A.op("ACT", lambda sc: sc.activation(sgl4[:], h1b[:],
                         AF.Silu), reads=[R(h1b)], writes=[R(sgl4)])
                    A.op("PE", lambda te, lb3=lb3, ur3=ur3: te.matmul(
                        PS[1][0:128, 0:TB], lb3.ap, ur3.ap, start=True,
                        stop=True), reads=[lb3, ur3],
                        writes=[R(PS[1], 0, 128, 0, TB)])
                    A.op("DVE", lambda v: v.tensor_add(h3b[:], base3[:],
                         PS[1][0:128, 0:TB]), reads=[R(base3),
                         R(PS[1], 0, 128, 0, TB)], writes=[R(h3b)])
                    A.op("DVE", lambda v: v.tensor_mul(h3b[:], sgl4[:],
                         h3b[:]), reads=[R(sgl4), R(h3b)], writes=[R(h3b)])
                    A.op("POOL", lambda g, wrr=wrr: g.tensor_mul(awf[:],
                         h3b[:], wrr.ap), reads=[R(h3b), wrr], writes=[R(awf)])
                    if e == 0:
                        A.op("DVE", lambda v: v.tensor_copy(mixf[:], awf[:]),
                             reads=[R(awf)], writes=[R(mixf)])
                    else:
                        A.op("DVE", lambda v: v.tensor_add(mixf[:], mixf[:],
                             awf[:]), reads=[R(mixf), R(awf)], writes=[R(mixf)])
                    A.op("ACT", lambda sc: sc.copy(awm[:], awf[:]),
                         reads=[R(awf)], writes=[R(awm)])
                    la2 = R(a2s, 0, 128, fc * 128 + e * 16, 16)
                    A.op("PE", lambda te, la2=la2: te.matmul(PS[2][0:16, 0:TB],
                         la2.ap, awm[:], start=True, stop=True),
                         reads=[la2, R(awm)], writes=[R(PS[2], 0, 16, 0, TB)])
                    A.op("DVE", lambda v, zr=zr: v.tensor_add(zr.ap, zr.ap,
                         PS[2][0:16, 0:TB]), reads=[zr, R(PS[2], 0, 16, 0, TB)],
                         writes=[zr])
                ff_ = R(ffa, 0, 128, icl * TB, TB)
                A.op("ACT", lambda sc, ff_=ff_: sc.copy(ff_.ap, mixf[:]),
                     reads=[R(mixf)], writes=[ff_])
            for dg in range(NDG):
                db = R(wbuf, 0, 128, (dg % 2) * 4096, 11 * 128)
                A.dma("POOL", db, fw2[:, (grp * NDG + dg) * 1408:
                                      (grp * NDG + dg + 1) * 1408])
                psd = PS[5 + (dg % 2)]
                for icl in range(11):
                    lw = R(wbuf, 0, 128, (dg % 2) * 4096 + icl * 128, 128)
                    ff_ = R(ffa, 0, 128, icl * TB, TB)
                    A.op("PE", lambda te, lw=lw, ff_=ff_, psd=psd, st0=icl == 0,
                         sp=icl == 10: te.matmul(psd[0:128, 0:TB], lw.ap,
                         ff_.ap, start=st0, stop=sp), reads=[lw, ff_],
                         writes=[R(psd, 0, 128, 0, TB)])
                far = R(ffacc, 0, 128, dg * TB, TB)
                if grp == 0:
                    A.op("DVE", lambda v, far=far, psd=psd: v.tensor_copy(
                        far.ap, psd[0:128, 0:TB]), reads=[R(psd, 0, 128, 0, TB)],
                        writes=[far])
                else:
                    A.op("DVE", lambda v, far=far, psd=psd: v.tensor_add(
                        far.ap, far.ap, psd[0:128, 0:TB]),
                        reads=[far, R(psd, 0, 128, 0, TB)], writes=[far])
        for e in range(8):
            zr = R(zacc, 0, 16, e * TB, TB)
            zb_ = R(zbe, 0, 16, e * TB, TB)
            A.op("DVE", lambda v, zr=zr, zb_=zb_: v.tensor_scalar_mul(zb_.ap,
                 zr.ap, 2.0), reads=[zr], writes=[zb_])
        for dg in range(NDG):
            A.dma("POOL", R(b2b), b2h[:, dg * 1024:(dg + 1) * 1024])
            psd = PS[5 + (dg % 2)]
            for e in range(8):
                lb2 = R(b2b, 0, 16, e * 128, 128)
                zb_ = R(zbe, 0, 16, e * TB, TB)
                A.op("PE", lambda te, lb2=lb2, zb_=zb_, psd=psd, st0=e == 0,
                     sp=e == 7: te.matmul(psd[0:128, 0:TB], lb2.ap, zb_.ap,
                     start=st0, stop=sp), reads=[lb2, zb_],
                     writes=[R(psd, 0, 128, 0, TB)])
            far = R(ffacc, 0, 128, dg * TB, TB)
            A.op("DVE", lambda v, far=far, psd=psd: v.tensor_add(far.ap,
                 far.ap, psd[0:128, 0:TB]), reads=[far, R(psd, 0, 128, 0, TB)],
                 writes=[far])
        for dg in range(NDG):
            xr = R(xrl, 0, 128, (dg % 2) * TB, TB)
            far = R(ffacc, 0, 128, dg * TB, TB)
            A.dma("SP", xr, x3d[dg * 128:(dg + 1) * 128, xlo:xlo + TB])
            A.op("DVE", lambda v, xr=xr, far=far: v.tensor_add(xr.ap, xr.ap,
                 far.ap), reads=[xr, far], writes=[xr])
            A.dma("SP", x4_o[dg * 128:(dg + 1) * 128, xlo:xlo + TB], xr)
    A.fence()
    return A


def stage_launch4(inputs, l3_results, x2_list):
    NFC = 44
    w = inputs
    w1h = _cc_layout(w["L1_attn_W1"])
    w3h = _cc_layout(w["L1_attn_W3"])
    w2h = _dn_layout(w["L1_attn_W2"], 4, 11)
    wo = _cc_layout(_pad_heads_rows(np.ascontiguousarray(w["L1_attn_Wo"]),
                                    96, 128))
    rms2 = np.zeros((128, NDG), np.float32)
    for dg in range(NDG):
        rms2[:, dg] = w["L1_attn_rms2_w"][dg * 128:(dg + 1) * 128]
    gnw = np.zeros((128, 1), np.float32)
    gnw[0:96, 0] = w["L1_attn_gnorm_w"]
    gwt = _cc_layout(np.ascontiguousarray(w["L1_ffn_gate_w"].T), 8)
    fw1 = _cc_layout(np.ascontiguousarray(w["L1_ffn_w1"].T))
    fw3 = _cc_layout(np.ascontiguousarray(w["L1_ffn_w3"].T))
    fw2 = _dn_layout(np.ascontiguousarray(w["L1_ffn_w2"].T), 4, 11)
    a1i = _cc_layout(np.concatenate([w["L1_ffn_a1"][e].T for e in range(8)],
                                    axis=1))
    a3i = _cc_layout(np.concatenate([w["L1_ffn_a3"][e].T for e in range(8)],
                                    axis=1))
    a2i = _cc_layout(np.concatenate([w["L1_ffn_a2"][e].T for e in range(8)],
                                    axis=1))
    b1c = np.zeros((16, NFC * 8 * 128), np.float32)
    b3c = np.zeros((16, NFC * 8 * 128), np.float32)
    for fc in range(NFC):
        for e in range(8):
            b1c[:, (fc * 8 + e) * 128:(fc * 8 + e + 1) * 128] = \
                w["L1_ffn_b1"][e].T[:, fc * 128:(fc + 1) * 128]
            b3c[:, (fc * 8 + e) * 128:(fc * 8 + e + 1) * 128] = \
                w["L1_ffn_b3"][e].T[:, fc * 128:(fc + 1) * 128]
    b2h = np.zeros((16, NDG * 8 * 128), np.float32)
    for dg in range(NDG):
        for e in range(8):
            b2h[:, (dg * 8 + e) * 128:(dg * 8 + e + 1) * 128] = \
                w["L1_ffn_b2"][e].T[:, dg * 128:(dg + 1) * 128]
    sel32 = np.zeros((NH, NH * 128), np.float32)
    for h in range(NH):
        sel32[h, h * 128:(h + 1) * 128] = 1.0
    sel8 = np.zeros((8, 8 * 128), np.float32)
    for e in range(8):
        sel8[e, e * 128:(e + 1) * 128] = 1.0
    sel8 = sel8.astype(BF)
    ltris = np.tril(np.ones((128, 128), np.float32), -1)
    utrii = np.triu(np.ones((128, 128), np.float32))
    idb = np.eye(128, dtype=np.float32).astype(BF)
    idf = np.eye(128, dtype=np.float32)
    el128 = np.zeros((128, 1), np.float32)
    el128[127, 0] = 1.0
    onesc = np.ones((128, 1), np.float32)
    onesr = np.ones((1, 128), np.float32)

    feeds = []
    for c in range(NCORES):
        ghops = np.zeros((1, 7 * NH), np.float32)
        lhops = np.zeros((128, 7 * 2048), np.float32)
        for hop in range(c):
            ghops[0, hop * NH:(hop + 1) * NH] = l3_results[hop]["glast"][:, 0]
            lhops[:, hop * 2048:(hop + 1) * 2048] = l3_results[hop]["sl"]
        feeds.append(dict(
            x2=x2_list[c], qn=_act_chunks(l3_results[c]["qn"]),
            kn=_act_chunks(l3_results[c]["kn"]),
            vo=_act_chunks(l3_results[c]["vo"]),
            go=_act_chunks(l3_results[c]["go"]),
            gcum=l3_results[c]["gcum"], beta=l3_results[c]["beta"],
            ghops=ghops, lhops=lhops, rms2=rms2, gnw=gnw, wo=wo,
            w1i=w1h, w3i=w3h, w2i=w2h,
            gw=gwt, fw1=fw1, fw3=fw3, fw2=fw2, a1i=a1i, a3i=a3i,
            b1c=b1c, b3c=b3c, a2i=a2i, b2h=b2h,
            onesc=onesc, onesr=onesr, sel32=sel32, sel8=sel8, ltris=ltris,
            utrii=utrii, idb=idb, idf=idf, el128=el128))
    return feeds


_nc_cache = {}


def _get_nc(name):
    if name not in _nc_cache:
        A = {"l1": build_launch1, "l2": build_launch2,
             "l3": build_launch3, "l4": build_launch4}[name]()
        _nc_cache[name] = A.finalize_build()
    return _nc_cache[name]


def kernel(**inputs):
    cores = list(range(NCORES))
    r1 = run_bass_kernel_spmd(_get_nc("l1"), stage_launch1(inputs),
                              core_ids=cores)
    l1res = [{k: np.asarray(v) for k, v in r.items()} for r in r1.results]
    r2 = run_bass_kernel_spmd(_get_nc("l2"), stage_launch2(inputs, l1res),
                              core_ids=cores)
    x2_list = [np.ascontiguousarray(np.asarray(r2.results[c]["x2"],
               np.float32)) for c in range(NCORES)]
    r3 = run_bass_kernel_spmd(_get_nc("l3"), stage_launch3(inputs, x2_list),
                              core_ids=cores)
    l3res = [{k: np.asarray(r3.results[c][k]) for k in
              ("qn", "kn", "vo", "go", "gcum", "beta", "glast", "sl")}
             for c in range(NCORES)]
    r4 = run_bass_kernel_spmd(_get_nc("l4"), stage_launch4(inputs, l3res,
                              x2_list), core_ids=cores)
    out = np.concatenate([np.asarray(r4.results[c]["x4"], np.float32).T
                          for c in range(NCORES)], axis=0)
    return np.ascontiguousarray(out)[None]
